# version 01
# baseline (speedup 1.0000x reference)
import numpy as np
import ml_dtypes
import concourse.bass as bass
import concourse.mybir as mybir
from concourse.bass_utils import run_bass_kernel_spmd
from concourse.alu_op_type import AluOpType as OP

F32 = mybir.dt.float32
BF16 = mybir.dt.bfloat16
AF = mybir.ActivationFunctionType
AX = mybir.AxisListType

L, D = 2048, 4096
H, P, NS, G = 128, 64, 128, 8
INTER = H * P
CONVC = INTER + 2 * G * NS
PROJ = INTER + CONVC + H
CS = 128
T = 512
NPASS = L // T
NCH = T // CS
EPS = 1e-5
NCB = PROJ // 128
HPG = H // G


def _inproj_sections():
    out = [("dt", INTER + CONVC, 128)]
    for g in range(G):
        out.append(("B", 2 * INTER + g * NS, NS))
        out.append(("C", 2 * INTER + G * NS + g * NS, NS))
        out.append(("hs", INTER + g * 1024, 1024))
        out.append(("gate", g * 1024, 1024))
    return out


class Sem:
    def __init__(self, h):
        self.h = h
        self.v = 0


class KB:
    ENGS = ("sync", "gpsimd", "vector", "scalar", "tensor")

    def __init__(self):
        self.glob = []
        self.last_wait = {e: {} for e in self.ENGS}

    def emit(self, eng, fn, sem=None, dn=1):
        if sem is None:
            self.glob.append((eng, fn))
            return None
        sem.v += dn
        v = sem.v

        def wrapped(E, fn=fn, s=sem.h, dn=dn):
            fn(E).then_inc(s, dn)

        self.glob.append((eng, wrapped))
        return v

    def wait(self, eng, sem, val):
        if val is None or val <= 0:
            return
        if getattr(self, "paranoid", False) and sem.v > val and sem.h.name in ("SW", "SG"):
            val = sem.v
        assert val <= sem.v, f"{eng}: wait on future milestone {val} > {sem.v}"
        lw = self.last_wait[eng]
        if lw.get(id(sem), -1) >= val:
            return
        lw[id(sem)] = val
        self.glob.append((eng, lambda E, s=sem.h, v=val: E.wait_ge(s, v)))

    def prog_for(self, eng, limit=None):
        items = self.glob if limit is None else self.glob[:limit]
        return [fn for (e, fn) in items if e == eng]


def build_nc(dbg=(), do_build=True, n_pass=NPASS, n_group=G, do_tail=True, limit=None, races=True, paranoid=False):
    nc = bass.Bass("TRN2", target_bir_lowering=False, detect_race_conditions=races)
    dbg = set(dbg)

    x_in = nc.dram_tensor("x", [L, D], F32, kind="ExternalInput").ap()
    ipw = nc.dram_tensor("ipw", [PROJ, D], F32, kind="ExternalInput").ap()
    convw = nc.dram_tensor("convw", [CONVC, 4], F32, kind="ExternalInput").ap()
    convb = nc.dram_tensor("convb", [CONVC], F32, kind="ExternalInput").ap()
    dtb = nc.dram_tensor("dtb", [H], F32, kind="ExternalInput").ap()
    alog = nc.dram_tensor("alog", [H], F32, kind="ExternalInput").ap()
    dvec = nc.dram_tensor("dvec", [H], F32, kind="ExternalInput").ap()
    normw = nc.dram_tensor("normw", [INTER], F32, kind="ExternalInput").ap()
    outw = nc.dram_tensor("outw", [D, INTER], F32, kind="ExternalInput").ap()
    ffw = nc.dram_tensor("ffw", [D, D], F32, kind="ExternalInput").ap()
    ffb = nc.dram_tensor("ffb", [D], F32, kind="ExternalInput").ap()
    ffA = nc.dram_tensor("ffA", [64, D], F32, kind="ExternalInput").ap()
    ffB = nc.dram_tensor("ffB", [D, 64], F32, kind="ExternalInput").ap()
    gw = nc.dram_tensor("gw", [4, D], F32, kind="ExternalInput").ap()
    gb = nc.dram_tensor("gb", [4], F32, kind="ExternalInput").ap()
    out = nc.dram_tensor("out", [L, D], F32, kind="ExternalOutput").ap()

    ipT = nc.dram_tensor("ipT", [NCB, 128, 32, 128], BF16, kind="Internal").ap()
    owT = nc.dram_tensor("owT", [8, 8, 128, 8, 512], BF16, kind="Internal").ap()
    wfT = nc.dram_tensor("wfT", [8, 4, 128, 8, 512], BF16, kind="Internal").ap()
    owP = nc.dram_tensor("owP", [D, INTER], BF16, kind="Internal").ap()
    wfP = nc.dram_tensor("wfP", [D, D], BF16, kind="Internal").ap()
    agT = nc.dram_tensor("agT", [128, 32, 68], BF16, kind="Internal").ap()
    bbT = nc.dram_tensor("bbT", [65, D], BF16, kind="Internal").ap()
    xnScr = nc.dram_tensor("xnScr", [T, D], BF16, kind="Internal").ap()
    hsScr = nc.dram_tensor("hsScr", [1024, T], BF16, kind="Internal").ap()
    bScr = nc.dram_tensor("bScr", [128, T], BF16, kind="Internal").ap()
    gScr = nc.dram_tensor("gScr", [64, 128, T], BF16, kind="Internal").ap()

    dbg_t = {}
    for name, shape in [("xnT", [32, 128, T]), ("cumH", [128, T]), ("proj", [NCB, 128, T]),
                        ("dtH", [128, T]), ("dA", [128, T]), ("dtraw", [128, T]),
                        ("bct", [256, T]), ("xnS", [T, D]), ("lnstat", [NCH, 128, 512]), ("ffng", [NCH, 128, 512]),
                        ("xch", [NCH, 128, 1024]), ("gateT", [8, 128, T]),
                        ("gT", [64, 128, T]), ("gTn", [64, 128, T])]:
        if name in dbg:
            dbg_t[name] = nc.dram_tensor("dbg_" + name, shape, F32, kind="ExternalOutput").ap()

    from contextlib import ExitStack
    es = ExitStack()
    S = lambda *a: es.enter_context(nc.sbuf_tensor(*a))
    Ps = lambda *a: es.enter_context(nc.psum_tensor(*a))

    xnT = S("xnT", [128, 32, 512], BF16)
    ggT = S("ggT", [128, 8, 512], BF16)
    gslab = S("gslab", [128, 4, 4, 512], BF16)
    state = S("state", [128, H, 64], F32)
    slab = S("slab", [128, 4, 2048], BF16)
    xLN = S("xLN", [128, 2048], F32)
    xnStg = S("xnStg", [128, 2048], BF16)
    cumH = S("cumH", [128, 512], F32)
    cumT = S("cumT", [128, NCH, 128], F32)
    dtaT = S("dtaT", [128, NCH, 128], F32)
    wTt = S("wTt", [128, NCH, 128], F32)
    ecumT = S("ecumT", [128, NCH, 128], F32)
    ECLs = S("ECLs", [128, NCH, 128], F32)
    dtH = S("dtH", [128, 512], F32)
    zb1 = S("zb1", [128, 512], F32)
    zb2 = S("zb2", [128, 512], F32)
    zb3 = S("zb3", [128, 512], F32)
    BCt = S("BCt", [128, 2, 512], BF16)
    Btok = S("Btok", [128, 2, 128], BF16)
    gateT = S("gateT", [128, 8, 512], BF16)
    hsS = S("hsS", [128, 1, 512], BF16)
    cvX = S("cvX", [128, 1, 515], F32)
    cvH = S("cvH", [128, 80, 3], F32)
    x_ch = S("x_ch", [128, 2, 1024], BF16)
    xwB = S("xwB", [128, 2, 64], BF16)
    prevb = S("prevb", [128, 2, 64], BF16)
    LdR = S("LdR", [128, 128], F32)
    Ldec = S("Ldec", [128, 2, 128], F32)
    CBmS = S("CBmS", [128, 2, 128], BF16)
    scoS = S("scoS", [128, 2, 128], BF16)
    y1B = S("y1B", [128, 64], F32)
    yB = S("yB", [128, 2, 64], F32)
    colB = S("colB", [128, 2, 128], F32)
    DBCs = S("DBCs", [128, 128], F32)
    g2b = S("g2b", [128, 512], BF16)
    rstdR = S("rstdR", [1, 512], F32)
    mrgS = S("mrgS", [128, 512], F32)
    mrgO = S("mrgO", [128, 2, 512], F32)
    xmrg = S("xmrg", [128, 2, 512], F32)
    tSf = S("tSf", [128, 64], F32)
    tS2 = S("tS2", [128, 64], F32)
    scGb = S("scGb", [128, 4], F32)
    mxG = S("mxG", [128, 1], F32)
    foG = S("foG", [128, 8], F32)
    tpT = S("tpT", [65, NCH, 128], BF16)
    r65 = S("r65", [65, 2, 512], BF16)
    bbF = S("bbF", [128, 65], F32)
    identf = S("identf", [128, 128], F32)
    ones1 = S("ones1", [1, 128], F32)
    onescb = S("onescb", [128, 1], BF16)
    trilb = S("trilb", [128, 128], BF16)
    rstpat = S("rstpat", [128, 512], BF16)
    onesF = S("onesF", [128, 128], F32)
    cwC = S("cwC", [128, 4, 80], F32)
    convbC = S("convbC", [128, 80], F32)
    normwC = S("normwC", [128, 64], F32)
    dtbC = S("dtbC", [128, 1], F32)
    AcolC = S("AcolC", [128, 1], F32)
    DrowS = S("DrowS", [1, 128], F32)
    gbR = S("gbR", [1, 4], F32)
    GBs = S("GBs", [128, 4], F32)
    explR = S("explR", [1, 128], F32)
    ecol = S("ecol", [128, 1], F32)
    statC = S("statC", [128, 8], F32)
    stats6 = S("stats6", [128, 8, 6], F32)
    meanvar = S("meanvar", [128, 2], F32)
    junkB = S("junkB", [128, 2048], BF16)
    epsC = S("epsC", [128, 1], F32)
    oneC = S("oneC", [128, 1], F32)
    ij = S("ij", [128, 128], F32)

    tapS = S("tapS", [128, 512], F32) if dbg else None
    p = [Ps("p%d" % i, [128, 512], F32) for i in range(8)]

    SW = Sem(es.enter_context(nc.semaphore("SW")))
    SG = Sem(es.enter_context(nc.semaphore("SG")))
    TE = Sem(es.enter_context(nc.semaphore("TE")))
    VE = Sem(es.enter_context(nc.semaphore("VE")))
    AC = Sem(es.enter_context(nc.semaphore("AC")))
    PO = Sem(es.enter_context(nc.semaphore("PO")))

    kb = KB()
    kb.paranoid = paranoid
    bank_guard = [[] for _ in range(8)]

    def bank_w(i):
        g = bank_guard[i]
        bank_guard[i] = []
        return g

    def bank_r(i, sem, ms):
        if ms:
            bank_guard[i].append((sem, ms))

    def dma(eng, dst, src, sem, waits=(), transpose=False):
        for (s, v) in waits:
            kb.wait(eng, s, v)
        if transpose:
            return kb.emit(eng, lambda X, d=dst, s2=src: X.dma_start(d, s2, transpose=True), sem, 16)
        return kb.emit(eng, lambda X, d=dst, s2=src: X.dma_start(d, s2), sem, 16)

    def mm(dst, lhsT, rhs, start, stop, waits=()):
        for (s, v) in waits:
            kb.wait("tensor", s, v)
        return kb.emit("tensor", lambda X, d=dst, l=lhsT, r=rhs, st=start, sp=stop:
                       X.matmul(d, l, r, start=st, stop=sp), TE)

    def tr(dst, src, ident, waits=()):
        for (s, v) in waits:
            kb.wait("tensor", s, v)
        return kb.emit("tensor", lambda X, d=dst, s2=src, i=ident: X.transpose(d, s2, i), TE)

    def vo(fn, waits=()):
        for (s, v) in waits:
            kb.wait("vector", s, v)
        return kb.emit("vector", fn, VE)

    def ao(fn, waits=()):
        for (s, v) in waits:
            kb.wait("scalar", s, v)
        return kb.emit("scalar", fn, AC)

    def po(fn, waits=()):
        for (s, v) in waits:
            kb.wait("gpsimd", s, v)
        return kb.emit("gpsimd", fn, PO)

    tap_guard = [[]]

    def tap(dst_ap, src_ap, waits=()):
        vv = vo(lambda X, d=src_ap: X.tensor_copy(tapS[:, 0:d.shape[-1]] if False else tapS[:], d),
                waits=list(waits) + tap_guard[0])
        dd = dma("gpsimd", dst_ap, tapS[:], SG, waits=[(VE, vv)])
        tap_guard[0] = [(SG, dd)]

    g_iota = po(lambda X: X.iota(ij[:], [[1, 128]], base=0, channel_multiplier=-1,
                                 allow_small_or_imprecise_dtypes=True))
    v_identf = vo(lambda X: X.tensor_scalar(identf[:], ij[:], 0.0, None, OP.is_equal),
                  waits=[(PO, g_iota)])
    vo(lambda X: X.tensor_scalar(zb1[:, 0:128], ij[:], 0.0, None, OP.is_ge))
    vo(lambda X: X.tensor_copy(trilb[:], zb1[:, 0:128]))
    vo(lambda X: X.memset(ones1[:], 1.0))
    vo(lambda X: X.memset(onesF[:], 1.0))
    vo(lambda X: X.memset(onescb[:], 1.0))
    vo(lambda X: X.memset(rstpat[:], 1.0))
    for c in range(NCH):
        vo(lambda X, c=c: X.memset(rstpat[:, c * CS:c * CS + 1], 0.0))
    vo(lambda X: X.memset(state[:], 0.0))
    vo(lambda X: X.memset(cvH[:], 0.0))
    vo(lambda X: X.memset(epsC[:], EPS))
    vo(lambda X: X.memset(oneC[:], 1.0))
    v_static = vo(lambda X: X.memset(tpT[64:65, :, :], 1.0))

    cwL = xLN[0:80, 0:512]
    cbL = xLN[0:80, 512:640]
    nwL = xLN[0:64, 640:768]
    d_cw = dma("sync", cwL, convw.rearrange("(t p) k -> t (p k)", p=128), SW)
    d_cb = dma("sync", cbL, convb.rearrange("(t p) -> t p", p=128), SW)
    d_nw = dma("sync", nwL, normw.rearrange("(t p) -> t p", p=128), SW)
    d_dtb = dma("sync", dtbC[:], dtb.rearrange("(p one) -> p one", one=1), SW)
    d_al = dma("sync", ecol[:], alog.rearrange("(p one) -> p one", one=1), SW)
    d_dr = dma("sync", DrowS[:], dvec.rearrange("(one p) -> one p", one=1), SW)
    d_gb = dma("sync", gbR[:], gb.rearrange("(one p) -> one p", one=1), SW)

    t_cw = None
    for k in range(4):
        t_cw = tr(p[2][:, k * 80:(k + 1) * 80],
                  cwL.rearrange("t (part k) -> t k part", k=4)[:, k, :],
                  identf[0:80, 0:80], waits=[(VE, v_identf), (SW, d_cw)])
    v_cw = vo(lambda X: X.tensor_copy(cwC.rearrange("part k t -> part (k t)"), p[2][:, 0:320]),
              waits=[(TE, t_cw)])
    bank_r(2, VE, v_cw)
    t_cb = tr(p[3][:, 0:80], cbL, identf[0:80, 0:80], waits=[(SW, d_cb)])
    t_nw = tr(p[3][:, 128:192], nwL, identf[0:64, 0:64], waits=[(SW, d_nw)])
    v_cb = vo(lambda X: X.tensor_copy(convbC[:], p[3][:, 0:80]), waits=[(TE, t_nw)])
    v_nw = vo(lambda X: X.tensor_copy(normwC[:], p[3][:, 128:192]), waits=[(TE, t_nw)])
    bank_r(3, VE, v_nw)
    a_A = ao(lambda X: X.activation(AcolC[:], ecol[:], AF.Exp), waits=[(SW, d_al)])
    v_A = vo(lambda X: X.tensor_scalar(AcolC[:], AcolC[:], -1.0, None, OP.mult),
             waits=[(AC, a_A)])
    t_db = mm(p[4][:, 0:128], ones1[:], DrowS[:], True, True,
              waits=[(SW, d_dr), (VE, v_static)])
    t_gb = mm(p[4][:, 128:132], ones1[:], gbR[:], True, True, waits=[(SW, d_gb)])
    v_db = vo(lambda X: X.tensor_copy(DBCs[:], p[4][:, 0:128]), waits=[(TE, t_gb)])
    v_gb = vo(lambda X: X.tensor_copy(GBs[:], p[4][:, 128:132]), waits=[(TE, t_gb)])
    bank_r(4, VE, v_gb)

    sections = _inproj_sections()
    cb_rows, cb_kind = [], []
    for (kind, start, n) in sections:
        for i in range(n // 128):
            cb_rows.append(start + i * 128)
            cb_kind.append(kind)
    assert len(cb_rows) == NCB

    ld_guard = [(TE, t_gb), (VE, v_nw), (VE, v_cb), (VE, v_cw)]
    if not do_build:
        ip_build_done = None
        ow_done = wf_done = ag_store = bb_store = None
        slab_guard = [[], []]
    do_build_sections = do_build
    slab_guard = [[], []]
    for cb in (range(NCB) if do_build_sections else []):
        row = cb_rows[cb]
        sp = cb % 2
        ms_cast = None
        for half in range(2):
            d1 = dma("sync", xLN[:, :], ipw[row:row + 128, half * 2048:(half + 1) * 2048],
                     SW, waits=ld_guard)
            ld_guard = []
            tms = {}
            for q in range(4):
                for b in range(4):
                    kk = q * 4 + b
                    tms[b] = tr(p[b][:, 0:128], xLN[:, kk * 128:(kk + 1) * 128],
                                identf[:], waits=[(SW, d1)] + bank_w(b))
                for b in range(4):
                    kk = half * 16 + q * 4 + b
                    wl = [(TE, tms[b])]
                    if half == 0 and q == 0 and b == 0:
                        wl += slab_guard[sp]
                        slab_guard[sp] = []
                    slot = 2 * sp + (kk // 16)
                    kc = kk % 16
                    ms_cast = vo(lambda X, slot=slot, kc=kc, b=b: X.tensor_copy(
                        slab[:, slot, kc * 128:(kc + 1) * 128], p[b][:, 0:128]), waits=wl)
                    bank_r(b, VE, ms_cast)
            ld_guard = [(TE, tms[3]), (VE, ms_cast)]
        st = dma("gpsimd", ipT[cb][:, 0:16, :],
                 slab[:, 2 * sp, :].rearrange("part (k c) -> part k c", c=128),
                 SG, waits=[(VE, ms_cast)])
        st = dma("gpsimd", ipT[cb][:, 16:32, :],
                 slab[:, 2 * sp + 1, :].rearrange("part (k c) -> part k c", c=128),
                 SG)
        slab_guard[sp] = [(SG, st)]
    ip_build_done = st if do_build_sections else None

    def cast_plain(src, dst, nrows, ncols, ld_guard):
        st = None
        for rt in range(nrows // 128):
            for half in range(ncols // 2048):
                d1 = dma("sync", xLN[:, :],
                         src[rt * 128:(rt + 1) * 128, half * 2048:(half + 1) * 2048],
                         SW, waits=ld_guard)
                msc = vo(lambda X: X.tensor_copy(xnStg[:], xLN[:]), waits=[(SW, d1)])
                st = dma("gpsimd", dst[rt * 128:(rt + 1) * 128, half * 2048:(half + 1) * 2048],
                         xnStg[:], SG, waits=[(VE, msc)])
                ld_guard = [(VE, msc), (SG, st)]
        return st, ld_guard

    ow_cast = wf_cast = None
    if do_build_sections:
        ow_cast, ld_guard = cast_plain(outw, owP, D, INTER, ld_guard)
        wf_cast, ld_guard = cast_plain(ffw, wfP, D, D, ld_guard)

    def build_tiled(srcP, dstT, n_ot, n_kq, cast_ms):
        last = None
        for ot in range(n_ot):
            for kq in range(n_kq):
                sp2 = (ot * n_kq + kq) % 2
                ms_l = None
                for k in range(8):
                    cblk = kq * 8 + k
                    wts = [(SG, cast_ms)]
                    if k == 0:
                        wts += slab_guard[sp2]
                        slab_guard[sp2] = []
                    slot = 2 * sp2 + (k // 4)
                    ms_l = dma("sync", slab[:, slot, (k % 4) * 512:(k % 4 + 1) * 512],
                               srcP[ot * 512:(ot + 1) * 512, cblk * 128:(cblk + 1) * 128],
                               SW, waits=wts, transpose=True)
                st = dma("gpsimd", dstT[ot, kq][:, 0:4, :],
                         slab[:, 2 * sp2, :].rearrange("part (k o) -> part k o", o=512),
                         SG, waits=[(SW, ms_l)])
                st = dma("gpsimd", dstT[ot, kq][:, 4:8, :],
                         slab[:, 2 * sp2 + 1, :].rearrange("part (k o) -> part k o", o=512),
                         SG)
                slab_guard[sp2] = [(SG, st)]
                last = st
        return last

    if do_build_sections:
        ow_done = build_tiled(owP, owT, 8, 8, ow_cast)
        wf_done = build_tiled(wfP, wfT, 8, 4, wf_cast)

    ag_store = None
    ag_guard = ld_guard
    for half in (range(2) if do_build_sections else []):
        dA1 = dma("sync", xLN[0:64, :], ffA[:, half * 2048:(half + 1) * 2048], SW,
                  waits=ag_guard)
        dA2 = dma("sync", xLN[64:68, :], gw[:, half * 2048:(half + 1) * 2048], SW)
        ag_guard = []
        msc = None
        for kk in range(16):
            k = half * 16 + kk
            wts = [(SW, dA2)] + bank_w(5)
            ag_tr = tr(p[5][:, 0:68], xLN[0:68, kk * 128:(kk + 1) * 128],
                       identf[0:68, 0:68], waits=wts)
            msc = vo(lambda X: X.tensor_copy(xnStg[0:128, 0:68], p[5][:, 0:68]),
                     waits=[(TE, ag_tr)] + ([(SG, ag_store)] if ag_store else []))
            bank_r(5, VE, msc)
            ag_store = dma("gpsimd", agT[:, k, :], xnStg[0:128, 0:68], SG, waits=[(VE, msc)])
        ag_guard = [(TE, ag_tr), (VE, msc)]

    bb_store = None
    bb_guard = []
    for ot2 in (range(32) if do_build_sections else []):
        dB1 = dma("sync", bbF[:, 0:64], ffB[ot2 * 128:(ot2 + 1) * 128, :], SW, waits=bb_guard)
        dB2 = dma("sync", bbF[:, 64:65],
                  ffb.rearrange("(part one) -> part one", one=1)[ot2 * 128:(ot2 + 1) * 128, :], SW)
        bb_tr = tr(p[6][0:65, 0:128], bbF[:], identf[:], waits=[(SW, dB2)] + bank_w(6))
        msc = vo(lambda X: X.tensor_copy(xnStg[0:65, 128:256], p[6][0:65, 0:128]),
                 waits=[(TE, bb_tr)] + ([(SG, bb_store)] if bb_store else []))
        bank_r(6, VE, msc)
        bb_store = dma("gpsimd", bbT[:, ot2 * 128:(ot2 + 1) * 128], xnStg[0:65, 128:256],
                       SG, waits=[(VE, msc)])
        bb_guard = [(TE, bb_tr)]

    xnT_guard = []
    gT_guard = []
    xLN_guard = bb_guard + ag_guard
    xnStg_guard = [(SG, bb_store)] if bb_store else []
    statC_guard = []
    zb_dt_guard = []
    cvx_guard = [[]]
    cvh_guard = {}
    zbc_guard = [[]]
    hss_guard = [[], []]
    bct_guard = [[], []]
    gate_guard = [[] for _ in range(8)]
    xch_guard = [[], []]
    btok_guard = [[], []]
    cbm_guard = [[], []]
    colb_guard = [[], []]
    ldec_guard = [[], []]
    sco_guard = [[], []]
    prevb_guard = [[], []]
    xw_guard = [[], []]
    yb_guard = [[], []]
    state_guard = {}
    g2_guard = []
    tsf_guard = []
    xmrg_guard = [[], []]
    mrgs_guard = []
    mrgo_guard = [[], []]
    r65_guard = [[], []]
    gslab_guard = [[], []]
    hsscr_guard = []
    bscr_guard = []

    zbc = [zb1, zb1]
    zbc2 = [zb2, zb2]

    for ps in range(n_pass):
        tok0 = ps * T

        xn_stores = []
        for tc in range(NCH):
            r0 = tok0 + tc * 128
            dx = dma("sync", xLN[:, :], x_in[r0:r0 + 128, 2048:4096], SW, waits=xLN_guard)
            xLN_guard = []
            v_sm1 = None
            for q in range(4):
                v_sm1 = vo(lambda X, q=q: X.bn_stats(stats6[:, 4 + q, :],
                                                     xLN[:, q * 512:(q + 1) * 512]),
                           waits=([(SW, dx)] + statC_guard) if q == 0 else [])
            statC_guard = []
            dx0 = dma("sync", xLN[:, :], x_in[r0:r0 + 128, 0:2048], SW,
                      waits=[(VE, v_sm1)])
            v_bn = None
            for q in range(4):
                v_bn = vo(lambda X, q=q: X.bn_stats(stats6[:, q, :], xLN[:, q * 512:(q + 1) * 512]),
                          waits=[(SW, dx0)] if q == 0 else [])
            v6 = vo(lambda X: X.bn_aggr(meanvar[:], stats6.rearrange("part a b -> part (a b)")),
                    waits=[(VE, v_bn)])
            a1 = ao(lambda X: X.activation(statC[:, 6:7], meanvar[:, 1:2], AF.Ln, bias=epsC[:]),
                    waits=[(VE, v6)])
            a2 = ao(lambda X: X.activation(statC[:, 7:8], statC[:, 6:7], AF.Exp, scale=-0.5))
            a_fence = ao(lambda X: X.activation(junkB[0:128, 0:2], statC[:, 6:8], AF.Relu))
            vps0 = vo(lambda X: X.tensor_scalar(xnStg[:], xLN[:], meanvar[:, 0:1], statC[:, 7:8],
                                                OP.subtract, OP.mult),
                      waits=[(AC, a_fence)] + xnStg_guard)
            xnStg_guard = []
            if "lnstat" in dbg:
                vt1 = vo(lambda X: X.tensor_copy(tapS[:, 0:2], meanvar[:]), waits=[(VE, v6)] + tap_guard[0])
                vt2 = vo(lambda X: X.tensor_copy(tapS[:, 2:4], statC[:, 6:8]), waits=[(AC, a_fence)])
                vt3 = vo(lambda X: X.tensor_copy(tapS[:, 4:52], stats6.rearrange("part a b -> part (a b)")))
                td = dma("gpsimd", dbg_t["lnstat"][tc], tapS[:], SG, waits=[(VE, vt3)])
                tap_guard[0] = [(SG, td)]
            st0 = dma("gpsimd", xnScr[tc * 128:(tc + 1) * 128, 0:2048], xnStg[:], SG,
                      waits=[(VE, vps0)])
            dx1 = dma("sync", xLN[:, :], x_in[r0:r0 + 128, 2048:4096], SW, waits=[(VE, vps0)])
            vps1 = vo(lambda X: X.tensor_scalar(xnStg[:], xLN[:], meanvar[:, 0:1], statC[:, 7:8],
                                                OP.subtract, OP.mult),
                      waits=[(SW, dx1), (SG, st0)])
            st1 = dma("gpsimd", xnScr[tc * 128:(tc + 1) * 128, 2048:4096], xnStg[:], SG,
                      waits=[(VE, vps1)])
            xn_stores.append(st1)
            xLN_guard = [(VE, vps1)]
            statC_guard = [(VE, vps1)]
            xnStg_guard = [(SG, st1)]

        if "xnS" in dbg:
            dma("gpsimd", dbg_t["xnS"], xnScr, SG, waits=[(SG, xn_stores[-1])])
        ms_xnT = None
        for f in range(32):
            wts = [(SG, xn_stores[-1])]
            if f == 0:
                wts += xnT_guard
                xnT_guard = []
            ms_xnT = dma("sync", xnT[:, f, :], xnScr[:, f * 128:(f + 1) * 128], SW,
                         waits=wts, transpose=True)
        if "xnT" in dbg:
            for f in range(32):
                tap(dbg_t["xnT"][f], xnT[:, f, :], waits=[(SW, ms_xnT)])

        sp = 0
        wts = [(SW, ms_xnT)] + slab_guard[sp]
        slab_guard[sp] = []
        d_h0 = dma("sync", slab[:, 2 * sp, :].rearrange("part (k c) -> part k c", c=128),
                   ipT[0][:, 0:16, :], SW, waits=wts)
        d_h1 = dma("sync", slab[:, 2 * sp + 1, :].rearrange("part (k c) -> part k c", c=128),
                   ipT[0][:, 16:32, :], SW)
        mmms = None
        wts = [(SW, d_h0)] + bank_w(0)
        for k in range(32):
            if k == 16:
                wts = [(SW, d_h1)]
            mmms = mm(p[0][:, 0:512],
                      slab[:, 2 * sp + k // 16, (k % 16) * 128:(k % 16 + 1) * 128],
                      xnT[:, k, :], k == 0, k == 31,
                      waits=wts if k in (0, 16) else [])
        slab_guard[sp] = [(TE, mmms)]
        vz = vo(lambda X: X.tensor_scalar(zb1[:], p[0][:, 0:512], dtbC[:], None, OP.add),
                waits=[(TE, mmms)] + zb_dt_guard)
        zb_dt_guard = []
        bank_r(0, VE, vz)
        aab = ao(lambda X: X.activation(zb2[:], zb1[:], AF.Abs), waits=[(VE, vz)])
        aex = ao(lambda X: X.activation(zb3[:], zb2[:], AF.Exp, scale=-1.0))
        aln = ao(lambda X: X.activation(zb2[:], zb3[:], AF.Ln, bias=oneC[:]))
        arl = ao(lambda X: X.activation(zb3[:], zb1[:], AF.Relu))
        vdt = vo(lambda X: X.tensor_tensor(dtH[:], zb2[:], zb3[:], OP.add), waits=[(AC, arl)])
        vda = vo(lambda X: X.tensor_scalar(zb1[:], dtH[:], AcolC[:], None, OP.mult),
                 waits=[(VE, v_A)])
        vcm = vo(lambda X: X.tensor_tensor_scan(cumH[:], rstpat[:], zb1[:], 0.0, OP.mult, OP.add))
        if "dtraw" in dbg:
            tap(dbg_t["dtraw"], p[0][:, 0:512], waits=[(TE, mmms)])
        if "dtH" in dbg:
            tap(dbg_t["dtH"], dtH[:], waits=[(VE, vdt)])
        if "dA" in dbg:
            tap(dbg_t["dA"], zb1[:], waits=[(VE, vda)])
        chunk_rdy = {}
        e3 = None
        for c in range(NCH):
            adl = ao(lambda X, c=c: X.activation(zb2[:, c * 128:(c + 1) * 128],
                                                 cumH[:, c * 128:(c + 1) * 128], AF.Exp,
                                                 scale=-1.0,
                                                 bias=cumH[:, c * 128 + 127:c * 128 + 128]),
                     waits=[(VE, vcm)])
            vw = vo(lambda X, c=c: X.tensor_tensor(zb3[:, c * 128:(c + 1) * 128],
                                                   dtH[:, c * 128:(c + 1) * 128],
                                                   zb2[:, c * 128:(c + 1) * 128], OP.mult),
                    waits=[(AC, adl)])
            t1 = tr(p[1][:, 0:128], cumH[:, c * 128:(c + 1) * 128], identf[:],
                    waits=[(VE, vcm)] + bank_w(1))
            e1 = vo(lambda X, c=c: X.tensor_copy(cumT[:, c, :], p[1][:, 0:128]), waits=[(TE, t1)])
            bank_r(1, VE, e1)
            t2 = tr(p[1][:, 128:256], dtH[:, c * 128:(c + 1) * 128], identf[:],
                    waits=[(VE, vdt)] + bank_w(1))
            e2 = vo(lambda X, c=c: X.tensor_copy(dtaT[:, c, :], p[1][:, 128:256]), waits=[(TE, t2)])
            bank_r(1, VE, e2)
            t3 = tr(p[1][:, 256:384], zb3[:, c * 128:(c + 1) * 128], identf[:],
                    waits=[(VE, vw)] + bank_w(1))
            e3 = vo(lambda X, c=c: X.tensor_copy(wTt[:, c, :], p[1][:, 256:384]), waits=[(TE, t3)])
            bank_r(1, VE, e3)
            a_ecm = ao(lambda X, c=c: X.activation(ecumT[:, c, :], cumT[:, c, :], AF.Exp),
                       waits=[(VE, e1)])
            ael = ao(lambda X, c=c: X.activation(ecol[:], cumH[:, c * 128 + 127:c * 128 + 128],
                                                 AF.Exp), waits=[(VE, vcm), (VE, v_A)])
            t4 = tr(p[1][0:1, 384:512], ecol[:], identf[:], waits=[(AC, ael)] + bank_w(1))
            e4 = vo(lambda X: X.tensor_copy(explR[:], p[1][0:1, 384:512]), waits=[(TE, t4)])
            bank_r(1, VE, e4)
            t5 = mm(p[5][:, 0:128], ones1[:], explR[:], True, True,
                    waits=[(VE, e4)] + bank_w(5))
            e5 = vo(lambda X, c=c: X.tensor_copy(ECLs[:, c, :], p[5][:, 0:128]), waits=[(TE, t5)])
            bank_r(5, VE, e5)
            chunk_rdy[c] = [(VE, e5), (AC, a_ecm)]
        dt_last = [(VE, e3), (VE, vcm)]
        zbc_guard[0] = [(VE, vw), (VE, vcm)] + ([(TE, t3)] if True else [])
        if "cumH" in dbg:
            tap(dbg_t["cumH"][:, 0:T], cumH[:], waits=[(VE, vcm)])

        cb_idx = 1
        rms_done = None
        lastW_xnT = None
        g_stores = []
        for g in range(n_group):
            conv_outs = {}
            hs_stores = []
            b_store = None
            gate_done = [None] * 8
            for (kind, ntile) in [("B", 1), ("C", 1), ("hs", 8), ("gate", 8)]:
                for it in range(ntile):
                    cb = cb_idx
                    cb_idx += 1
                    sp = cb % 2
                    pp = cb % 2
                    wts = list(slab_guard[sp])
                    slab_guard[sp] = []
                    d_h0 = dma("sync",
                               slab[:, 2 * sp, :].rearrange("part (k c) -> part k c", c=128),
                               ipT[cb][:, 0:16, :], SW, waits=wts)
                    d_h1 = dma("sync",
                               slab[:, 2 * sp + 1, :].rearrange("part (k c) -> part k c", c=128),
                               ipT[cb][:, 16:32, :], SW)
                    wts = [(SW, d_h0)] + bank_w(pp)
                    mmms = None
                    for k in range(32):
                        if k == 16:
                            wts = [(SW, d_h1)]
                        mmms = mm(p[pp][:, 0:512],
                                  slab[:, 2 * sp + k // 16, (k % 16) * 128:(k % 16 + 1) * 128],
                                  xnT[:, k, :], k == 0, k == 31,
                                  waits=wts if k in (0, 16) else [])
                    slab_guard[sp] = [(TE, mmms)]
                    if "proj" in dbg and ps == 0 and g == 0:
                        tap(dbg_t["proj"][cb], p[pp][:, 0:512], waits=[(TE, mmms)])
                        bank_r(pp, SG, SG.v)
                    if kind == "gate":
                        asl = ao(lambda X, it=it, pp=pp: X.activation(gateT[:, it, :], p[pp][:, 0:512],
                                                               AF.Silu),
                                 waits=[(TE, mmms)] + gate_guard[it])
                        gate_guard[it] = []
                        bank_r(pp, AC, asl)
                        gate_done[it] = asl
                    else:
                        proj_row = cb_rows[cb]
                        ctg = (proj_row - INTER) // 128
                        bp2 = 0
                        ve1 = vo(lambda X, b2=bp2, ctg=ctg, pp=pp: X.tensor_copy(
                            cvX[:, b2, 3:515], p[pp][:, 0:512]),
                            waits=[(TE, mmms), (VE, v_cb)] + cvx_guard[0])
                        cvx_guard[0] = []
                        bank_r(pp, VE, ve1)
                        ph1 = po(lambda X, b2=bp2, ctg=ctg: X.tensor_copy(
                            cvX[:, b2, 0:3], cvH[:, ctg, :]),
                            waits=[(VE, ve1)] + cvh_guard.get(ctg, []))
                        mac0 = vo(lambda X, b2=bp2, ctg=ctg: X.tensor_scalar(
                            zbc[b2][:, 0:512], cvX[:, b2, 0:512], cwC[:, 0, ctg:ctg + 1],
                            None, OP.mult),
                            waits=[(VE, v_cw), (PO, ph1)] + zbc_guard[0])
                        zbc_guard[0] = []
                        mac1 = vo(lambda X, b2=bp2, ctg=ctg: X.scalar_tensor_tensor(
                            zbc2[b2][:, 0:512], cvX[:, b2, 1:513], cwC[:, 1, ctg:ctg + 1],
                            zbc[b2][:, 0:512], OP.mult, OP.add))
                        mac2 = vo(lambda X, b2=bp2, ctg=ctg: X.scalar_tensor_tensor(
                            zbc[b2][:, 0:512], cvX[:, b2, 2:514], cwC[:, 2, ctg:ctg + 1],
                            zbc2[b2][:, 0:512], OP.mult, OP.add))
                        mac3 = vo(lambda X, b2=bp2, ctg=ctg: X.scalar_tensor_tensor(
                            zbc2[b2][:, 0:512], cvX[:, b2, 3:515], cwC[:, 3, ctg:ctg + 1],
                            zbc[b2][:, 0:512], OP.mult, OP.add))
                        ph2 = po(lambda X, b2=bp2, ctg=ctg: X.tensor_copy(
                            cvH[:, ctg, :], cvX[:, b2, 512:515]), waits=[(VE, mac3)])
                        cvh_guard[ctg] = [(PO, ph2)]
                        cvx_guard[0] = [(VE, mac3), (PO, ph2)]
                        if kind == "hs":
                            asl = ao(lambda X, it=it, b2=bp2, ctg=ctg: X.activation(
                                hsS[:, 0, :], zbc2[b2][:, 0:512], AF.Silu,
                                bias=convbC[:, ctg:ctg + 1]),
                                waits=[(VE, mac3)] + hss_guard[0])
                            hss_guard[0] = []
                            zbc_guard[0] = [(AC, asl)]
                            sth = dma("gpsimd", hsScr[it * 128:(it + 1) * 128, :],
                                      hsS[:, 0, :], SG,
                                      waits=[(AC, asl)] + (hsscr_guard if it == 0 else []))
                            if it == 0:
                                hsscr_guard = []
                            hs_stores.append(sth)
                            hss_guard[0] = [(SG, sth)]
                        else:
                            bc = 0 if kind == "B" else 1
                            asl = ao(lambda X, bc=bc, b2=bp2, ctg=ctg: X.activation(
                                BCt[:, bc, :], zbc2[b2][:, 0:512], AF.Silu,
                                bias=convbC[:, ctg:ctg + 1]),
                                waits=[(VE, mac3)] + bct_guard[bc])
                            bct_guard[bc] = []
                            zbc_guard[0] = [(AC, asl)]
                            conv_outs[kind] = asl
                            if kind == "B":
                                b_store = dma("gpsimd", bScr[:], BCt[:, 0, :], SG,
                                              waits=[(AC, asl)] + bscr_guard)
                                bscr_guard = []
            if "bct" in dbg and g == 0 and ps == 0:
                tap(dbg_t["bct"][0:128], BCt[:, 0, :], waits=[(AC, conv_outs["B"])])
                tap(dbg_t["bct"][128:256], BCt[:, 1, :], waits=[(AC, conv_outs["C"])])
            if "gateT" in dbg and g == 0 and ps == 0:
                for it in range(8):
                    tap(dbg_t["gateT"][it], gateT[:, it, :], waits=[(AC, gate_done[it])])

            last_gT = None
            last_dx = last_db = None
            last_tyd = last_pxw = last_vy1 = last_tst = None
            for c in range(NCH):
                cpar = c % 2
                wts = [(SG, hs_stores[-1])] + xch_guard[cpar]
                xch_guard[cpar] = []
                d_x = dma("sync", x_ch[:, cpar, :], hsScr[:, c * 128:(c + 1) * 128], SW,
                          waits=wts, transpose=True)
                wts = [(SG, b_store)] + btok_guard[cpar]
                btok_guard[cpar] = []
                d_b = dma("sync", Btok[:, cpar, :], bScr[:, c * 128:(c + 1) * 128], SW,
                          waits=wts, transpose=True)
                last_dx, last_db = d_x, d_b
                if "xch" in dbg and g == 0 and ps == 0:
                    tap(dbg_t["xch"][c, :, 0:512], x_ch[:, cpar, 0:512], waits=[(SW, d_x)])
                    tap(dbg_t["xch"][c, :, 512:1024], x_ch[:, cpar, 512:1024], waits=[(SW, d_x)])
                wcb = [(AC, conv_outs["C"]), (AC, conv_outs["B"])] + bank_w(2)
                t_cbm = mm(p[2][:, 0:128], BCt[:, 0, c * 128:(c + 1) * 128],
                           BCt[:, 1, c * 128:(c + 1) * 128], True, True, waits=wcb)
                v_cbm = vo(lambda X, cp=cpar: X.tensor_tensor(CBmS[:, cp, :], p[2][:, 0:128],
                                                              trilb[:], OP.mult),
                           waits=[(TE, t_cbm)] + cbm_guard[cpar])
                cbm_guard[cpar] = []
                bank_r(2, VE, v_cbm)
                for hl in range(HPG):
                    h = g * HPG + hl
                    hp = hl % 2
                    p_cb = po(lambda X, hp=hp, c=c, h=h: X.tensor_scalar(
                        colB[:, hp, :], onesF[:], cumT[:, c, h:h + 1], None, OP.mult),
                        waits=chunk_rdy[c] + colb_guard[hp])
                    colb_guard[hp] = []
                    t_R = tr(p[3][:, 0:128], colB[:, hp, :], identf[:],
                             waits=[(PO, p_cb)] + bank_w(3))
                    colb_guard[hp] = [(TE, t_R)]
                    a_r = ao(lambda X, c=c, h=h: X.activation(LdR[:], p[3][:, 0:128], AF.Relu,
                                                              scale=-1.0,
                                                              bias=cumT[:, c, h:h + 1]),
                             waits=[(TE, t_R)])
                    bank_r(3, AC, a_r)
                    a_e = ao(lambda X, hp=hp: X.activation(Ldec[:, hp, :], LdR[:], AF.Exp,
                                                           scale=-1.0), waits=ldec_guard[hp])
                    ldec_guard[hp] = []
                    v_sc = vo(lambda X, hp=hp, cp=cpar, c=c, h=h: X.scalar_tensor_tensor(
                        scoS[:, hp, :], CBmS[:, cp, :], dtaT[:, c, h:h + 1], Ldec[:, hp, :],
                        OP.mult, OP.mult),
                        waits=[(AC, a_e), (VE, v_cbm)] + sco_guard[hp])
                    sco_guard[hp] = []
                    ldec_guard[hp] = [(VE, v_sc)]
                    t_yd = mm(p[4][:, 0:64], scoS[:, hp, :],
                              x_ch[:, cpar, hl * 64:(hl + 1) * 64], True, True,
                              waits=[(VE, v_sc), (SW, d_x)] + bank_w(4))
                    sco_guard[hp] = [(TE, t_yd)]
                    last_tyd = t_yd
                    p_pv = po(lambda X, hp=hp, h=h: X.tensor_copy(prevb[:, hp, :], state[:, h, :]),
                              waits=prevb_guard[hp] + state_guard.get(h, []))
                    prevb_guard[hp] = []
                    t_yi = mm(p[5][:, 0:64], BCt[:, 1, c * 128:(c + 1) * 128], prevb[:, hp, :],
                              True, True, waits=[(PO, p_pv)] + bank_w(5))
                    prevb_guard[hp] = [(TE, t_yi)]
                    p_xw = po(lambda X, hp=hp, cp=cpar, c=c, h=h, hl=hl: X.tensor_scalar(
                        xwB[:, hp, :], x_ch[:, cp, hl * 64:(hl + 1) * 64], wTt[:, c, h:h + 1],
                        None, OP.mult), waits=xw_guard[hp] + [(SW, d_x)])
                    xw_guard[hp] = []
                    last_pxw = p_xw
                    t_st = mm(p[6][:, 0:64], Btok[:, cpar, :], xwB[:, hp, :], True, True,
                              waits=[(PO, p_xw), (SW, d_b)] + bank_w(6))
                    xw_guard[hp] = [(TE, t_st)]
                    last_tst = t_st
                    v_su = vo(lambda X, h=h, c=c: X.scalar_tensor_tensor(
                        state[:, h, :], state[:, h, :], ECLs[:, c, h:h + 1], p[6][:, 0:64],
                        OP.mult, OP.add), waits=[(TE, t_st), (PO, p_pv)])
                    bank_r(6, VE, v_su)
                    state_guard[h] = [(VE, v_su)]
                    v_y1 = vo(lambda X, cp=cpar, hl=hl, h=h: X.scalar_tensor_tensor(
                        y1B[:], x_ch[:, cp, hl * 64:(hl + 1) * 64], DBCs[:, h:h + 1],
                        p[4][:, 0:64], OP.mult, OP.add), waits=[(TE, t_yd), (VE, v_db)])
                    bank_r(4, VE, v_y1)
                    last_vy1 = v_y1
                    v_y = vo(lambda X, hp=hp, c=c, h=h: X.scalar_tensor_tensor(
                        yB[:, hp, :], p[5][:, 0:64], ecumT[:, c, h:h + 1], y1B[:],
                        OP.mult, OP.add), waits=[(TE, t_yi)] + yb_guard[hp])
                    yb_guard[hp] = []
                    bank_r(5, VE, v_y)
                    t_yt = tr(p[3][0:64, 128:256], yB[:, hp, :], identf[:],
                              waits=[(VE, v_y)] + bank_w(3))
                    yb_guard[hp] = [(TE, t_yt)]
                    gtile = hl // 2
                    goff = (hl % 2) * 64
                    v_g = vo(lambda X, gt=gtile, go=goff, hl=hl, c=c: X.tensor_tensor(
                        ggT[go:go + 64, gt, c * 128:(c + 1) * 128], p[3][0:64, 128:256],
                        gateT[go:go + 64, hl // 2, c * 128:(c + 1) * 128], OP.mult),
                        waits=[(TE, t_yt), (AC, gate_done[hl // 2])] + gT_guard)
                    gT_guard = []
                    bank_r(3, VE, v_g)
                    last_gT = v_g
                cbm_guard[cpar] = [(VE, v_sc)]
                xch_guard[cpar] = [(TE, last_tyd), (PO, last_pxw), (VE, last_vy1)]
                btok_guard[cpar] = [(TE, last_tst)]
            hsscr_guard = [(SW, last_dx)]
            bscr_guard = [(SW, last_db)]
            for it in range(8):
                gate_guard[it] = [(VE, last_gT)]
            bct_guard[0] = [(TE, last_tst)]
            bct_guard[1] = [(TE, max(t_cbm, t_yi))]

            t_v = None
            for ct in range(8):
                a_g2 = ao(lambda X, g=g, ct=ct: X.activation(g2b[:], ggT[:, ct, :],
                                                             AF.Square),
                          waits=[(VE, last_gT)] + g2_guard)
                g2_guard = []
                t_v = mm(p[2][0:1, 0:512], onescb[:], g2b[:], ct == 0, ct == 7,
                         waits=[(AC, a_g2)] + (bank_w(2) if ct == 0 else []))
                g2_guard = [(TE, t_v)]
            a_ln = ao(lambda X: X.activation(rstdR[:], p[2][0:1, 0:512], AF.Ln,
                                             scale=1.0 / 1024.0, bias=epsC[0:1, :]), waits=[(TE, t_v)])
            a_rs = ao(lambda X: X.activation(rstdR[:], rstdR[:], AF.Exp,
                                             scale=-0.5))
            bank_r(2, AC, a_ln)
            t_rb = mm(p[2][:, 0:512], ones1[:], rstdR[:], True, True,
                      waits=[(AC, a_rs)] + bank_w(2))
            vms = None
            for ct in range(8):
                vms = vo(lambda X, g=g, ct=ct: X.scalar_tensor_tensor(
                    ggT[:, ct, :], ggT[:, ct, :],
                    normwC[:, g * 8 + ct:g * 8 + ct + 1], p[2][:, 0:512], OP.mult, OP.mult),
                    waits=[(TE, t_rb), (VE, v_nw)])
            bank_r(2, VE, vms)
            rms_done = vms
            st_g = dma("gpsimd", gScr.rearrange("t part f -> part t f")[:, g * 8:(g + 1) * 8, :],
                       ggT[:], SG, waits=[(VE, vms)])
            g_stores.append(st_g)
            gT_guard = [(SG, st_g)]
            if "gT" in dbg and ps == 0:
                for ct in range(8):
                    tap(dbg_t["gT"][g * 8 + ct], ggT[:, ct, :], waits=[(VE, vms)])

        if not do_tail:
            continue
        wts = list(slab_guard[0]) + list(slab_guard[1])
        slab_guard[0] = []
        slab_guard[1] = []
        d_ag0 = dma("sync", slab[:, 0, 0:1088],
                    agT[:, 0:16, :].rearrange("part k c -> part (k c)"), SW,
                    waits=wts + ([(SG, ag_store)] if ag_store else []))
        d_agl = dma("sync", slab[:, 1, 0:1088],
                    agT[:, 16:32, :].rearrange("part k c -> part (k c)"), SW)
        gat_ms = {}
        last_gmm = None
        for tc in range(NCH):
            wts = [(SW, d_agl)] + bank_w(0)
            mmt = None
            for k in range(32):
                agsl = slab[:, k // 16, (k % 16) * 68:(k % 16 + 1) * 68]
                mmt = mm(p[0][:, 0:68], xnT[:, k, tc * 128:(tc + 1) * 128], agsl,
                         k == 0, k == 31, waits=wts if k == 0 else [])
            last_gmm = mmt
            v_t = vo(lambda X: X.tensor_copy(tSf[:], p[0][:, 0:64]),
                     waits=[(TE, mmt)] + tsf_guard)
            tsf_guard = []
            v_sg = vo(lambda X: X.tensor_tensor(scGb[:], p[0][:, 64:68], GBs[:], OP.add),
                      waits=[(VE, v_gb)])
            bank_r(0, VE, v_sg)
            v_mx = vo(lambda X: X.tensor_reduce(mxG[:], scGb[:], AX.X, OP.max),
                      waits=[(VE, v_sg)])
            pv = vo(lambda X: X.tensor_tensor(foG[:, 0:1], scGb[:, 0:1], mxG[:], OP.is_ge),
                    waits=[(VE, v_mx)])
            pv = vo(lambda X: X.tensor_scalar(foG[:, 4:5], foG[:, 0:1], -1.0, 1.0, OP.mult, OP.add),
                    waits=[(VE, pv)])
            pv = vo(lambda X: X.tensor_tensor(foG[:, 1:2], scGb[:, 1:2], mxG[:], OP.is_ge),
                    waits=[(VE, pv)])
            pv = vo(lambda X: X.tensor_tensor(foG[:, 1:2], foG[:, 1:2], foG[:, 4:5], OP.mult),
                    waits=[(VE, pv)])
            pv = vo(lambda X: X.tensor_tensor(foG[:, 5:6], foG[:, 4:5], foG[:, 1:2], OP.subtract),
                    waits=[(VE, pv)])
            pv = vo(lambda X: X.tensor_tensor(foG[:, 2:3], scGb[:, 2:3], mxG[:], OP.is_ge),
                    waits=[(VE, pv)])
            pv = vo(lambda X: X.tensor_tensor(foG[:, 2:3], foG[:, 2:3], foG[:, 5:6], OP.mult),
                    waits=[(VE, pv)])
            pv = vo(lambda X: X.tensor_tensor(foG[:, 6:7], foG[:, 5:6], foG[:, 2:3], OP.subtract),
                    waits=[(VE, pv)])
            vo(lambda X: X.tensor_copy(tS2[:, 0:8], tSf[:, 0:8]), waits=[(VE, pv)])
            vo(lambda X: X.tensor_scalar(tS2[:, 8:16], tSf[:, 8:16], foG[:, 4:5], None, OP.mult))
            vo(lambda X: X.tensor_scalar(tS2[:, 16:32], tSf[:, 16:32], foG[:, 5:6], None, OP.mult))
            v_m3 = vo(lambda X: X.tensor_scalar(tS2[:, 32:64], tSf[:, 32:64], foG[:, 6:7],
                                                None, OP.mult))
            tsf_guard = [(VE, v_m3)]
            if "ffng" in dbg and ps == 0:
                f1 = vo(lambda X: X.tensor_copy(tapS[:, 0:4], scGb[:]), waits=tap_guard[0])
                vo(lambda X: X.tensor_copy(tapS[:, 4:5], mxG[:]))
                vo(lambda X: X.tensor_copy(tapS[:, 8:16], foG[:]))
                vo(lambda X: X.tensor_copy(tapS[:, 16:80], tSf[:]))
                f5 = vo(lambda X: X.tensor_copy(tapS[:, 80:144], tS2[:]))
                td = dma("gpsimd", dbg_t["ffng"][tc], tapS[:], SG, waits=[(VE, f5)])
                tap_guard[0] = [(SG, td)]
            t_tp = tr(p[3][0:64, 256:384], tS2[:], identf[:], waits=[(VE, v_m3)] + bank_w(3))
            v_tp = vo(lambda X, tc=tc: X.tensor_copy(tpT[0:64, tc, :], p[3][0:64, 256:384]),
                      waits=[(TE, t_tp)])
            bank_r(3, VE, v_tp)
            gat_ms[tc] = v_tp
        slab_guard[0] = [(TE, last_gmm)]
        slab_guard[1] = [(TE, last_gmm)]

        for ot in range(8):
            xmrg_ms = {}
            for tc in range(2):
                wts = list(xmrg_guard[tc])
                xmrg_guard[tc] = []
                xmrg_ms[tc] = dma("sync", xmrg[:, tc, :],
                                  x_in[tok0 + tc * 128:tok0 + (tc + 1) * 128,
                                       ot * 512:(ot + 1) * 512], SW, waits=wts)
            mmo = None
            for kq in range(8):
                sp2 = (ot * 13 + kq) % 2
                sp3 = kq % 2
                wts = list(slab_guard[sp2])
                slab_guard[sp2] = []
                d_u0 = dma("sync",
                           slab[:, 2 * sp2, :].rearrange("part (k o) -> part k o", o=512),
                           owT[ot, kq][:, 0:4, :], SW, waits=wts + ([(SG, ow_done)] if ow_done else []))
                d_u = dma("sync",
                          slab[:, 2 * sp2 + 1, :].rearrange("part (k o) -> part k o", o=512),
                          owT[ot, kq][:, 4:8, :], SW)
                wts = list(gslab_guard[sp3])
                gslab_guard[sp3] = []
                d_g0 = dma("sync", gslab[:, 2 * sp3, :, :],
                           gScr.rearrange("t part f -> part t f")[:, kq * 8:kq * 8 + 4, :],
                           SW, waits=wts + [(SG, g_stores[-1])])
                d_g = dma("sync", gslab[:, 2 * sp3 + 1, :, :],
                          gScr.rearrange("t part f -> part t f")[:, kq * 8 + 4:kq * 8 + 8, :],
                          SW)
                for k8 in range(8):
                    for tc in range(NCH):
                        wl = [(SW, d_u0 if k8 < 4 else d_u), (SW, d_g0 if k8 < 4 else d_g)]
                        if kq == 0 and k8 == 0:
                            wl += bank_w(tc)
                        mmo = mm(p[tc][:, 0:512],
                                 gslab[:, 2 * sp3 + k8 // 4, k8 % 4, tc * 128:(tc + 1) * 128],
                                 slab[:, 2 * sp2 + k8 // 4, (k8 % 4) * 512:(k8 % 4 + 1) * 512],
                                 kq == 0 and k8 == 0, kq == 7 and k8 == 7, waits=wl)
                slab_guard[sp2] = [(TE, mmo)]
                gslab_guard[sp3] = [(TE, mmo)]
            mmw = None
            for kq in range(4):
                sp2 = (ot * 13 + 8 + kq) % 2
                wts = list(slab_guard[sp2])
                slab_guard[sp2] = []
                d_u0 = dma("sync",
                           slab[:, 2 * sp2, :].rearrange("part (k o) -> part k o", o=512),
                           wfT[ot, kq][:, 0:4, :], SW, waits=wts + ([(SG, wf_done)] if wf_done else []))
                d_u = dma("sync",
                          slab[:, 2 * sp2 + 1, :].rearrange("part (k o) -> part k o", o=512),
                          wfT[ot, kq][:, 4:8, :], SW)
                for k8 in range(8):
                    kk = kq * 8 + k8
                    for tc in range(NCH):
                        wl = [(SW, d_u0 if k8 < 4 else d_u)]
                        if kq == 0 and k8 == 0:
                            wl += bank_w(4 + tc)
                        mmw = mm(p[4 + tc][:, 0:512], xnT[:, kk, tc * 128:(tc + 1) * 128],
                                 slab[:, 2 * sp2 + k8 // 4, (k8 % 4) * 512:(k8 % 4 + 1) * 512],
                                 kq == 0 and k8 == 0, False, waits=wl)
                slab_guard[sp2] = [(TE, mmw)]
            lastW_xnT = mmw
            rp = ot % 2
            wts = list(r65_guard[rp])
            r65_guard[rp] = []
            d_r = dma("sync", r65[:, rp, :], bbT[:, ot * 512:(ot + 1) * 512], SW,
                      waits=wts + [(SG, bb_store)])
            mml = None
            for tc in range(NCH):
                mml = mm(p[4 + tc][:, 0:512], tpT[:, tc, :], r65[:, rp, :], False, True,
                         waits=[(SW, d_r), (VE, gat_ms[tc])])
            r65_guard[rp] = [(TE, mml)]
            for tc in range(NCH):
                mp = tc % 2
                a_si = ao(lambda X, tc=tc: X.activation(mrgS[:], p[4 + tc][:, 0:512], AF.Silu),
                          waits=[(TE, mml)] + mrgs_guard)
                mrgs_guard = []
                bank_r(4 + tc, AC, a_si)
                v_mu = vo(lambda X, tc=tc, mp=mp: X.tensor_tensor(
                    mrgO[:, mp, :], p[tc][:, 0:512], mrgS[:], OP.mult),
                    waits=[(AC, a_si), (TE, mmo)] + mrgo_guard[mp])
                mrgo_guard[mp] = []
                bank_r(tc, VE, v_mu)
                v_ad = vo(lambda X, tc=tc, mp=mp: X.tensor_tensor(
                    mrgO[:, mp, :], mrgO[:, mp, :], xmrg[:, tc % 2, :], OP.add),
                    waits=[(SW, xmrg_ms[tc])])
                if tc + 2 < NCH:
                    xmrg_ms[tc + 2] = dma("sync", xmrg[:, tc % 2, :],
                                          x_in[tok0 + (tc + 2) * 128:tok0 + (tc + 3) * 128,
                                               ot * 512:(ot + 1) * 512], SW,
                                          waits=[(VE, v_ad)])
                mrgs_guard = [(VE, v_mu)]
                st_o = dma("gpsimd", out[tok0 + tc * 128:tok0 + (tc + 1) * 128,
                                         ot * 512:(ot + 1) * 512],
                           mrgO[:, mp, :], SG, waits=[(VE, v_ad)])
                mrgo_guard[mp] = [(SG, st_o)]
                if tc >= 2:
                    xmrg_guard[tc % 2] = [(VE, v_ad)]
        xnT_guard = [(TE, lastW_xnT)]
        gT_guard = [(SW, SW.v)]
        zb_dt_guard = [(PO, PO.v), (AC, AC.v)]

    kb.wait("gpsimd", SG, SG.v)
    kb.wait("sync", SW, SW.v)

    with es, nc.Block() as block:
        @block.sync
        def _(E):
            for fn in kb.prog_for("sync", limit):
                fn(E)

        @block.gpsimd
        def _(E):
            for fn in kb.prog_for("gpsimd", limit):
                fn(E)

        @block.vector
        def _(E):
            for fn in kb.prog_for("vector", limit):
                fn(E)

        @block.scalar
        def _(E):
            for fn in kb.prog_for("scalar", limit):
                fn(E)

        @block.tensor
        def _(E):
            for fn in kb.prog_for("tensor", limit):
                fn(E)

    return nc


def kernel(**inputs):
    nc = build_nc(paranoid=True)
    x = np.ascontiguousarray(inputs["L0_attn_input_states"])
    base = {
        "ipw": np.ascontiguousarray(inputs["L0_attn_in_proj_w"]),
        "convw": np.ascontiguousarray(inputs["L0_attn_conv_w"][:, 0, :]),
        "convb": np.ascontiguousarray(inputs["L0_attn_conv_b"]),
        "dtb": np.ascontiguousarray(inputs["L0_attn_dt_bias"]),
        "alog": np.ascontiguousarray(inputs["L0_attn_A_log"]),
        "dvec": np.ascontiguousarray(inputs["L0_attn_D"]),
        "normw": np.ascontiguousarray(inputs["L0_attn_norm_w"]),
        "outw": np.ascontiguousarray(inputs["L0_attn_out_w"]),
        "ffw": np.ascontiguousarray(inputs["L0_ffn_W"]),
        "ffb": np.ascontiguousarray(inputs["L0_ffn_b"]),
        "ffA": np.ascontiguousarray(inputs["L0_ffn_A"]),
        "ffB": np.ascontiguousarray(inputs["L0_ffn_B"]),
        "gw": np.ascontiguousarray(inputs["L0_ffn_gate_w"]),
        "gb": np.ascontiguousarray(inputs["L0_ffn_gate_b"]),
    }
    in_maps = []
    for b in range(8):
        m = dict(base)
        m["x"] = np.ascontiguousarray(x[b])
        in_maps.append(m)
    res = run_bass_kernel_spmd(nc, in_maps, core_ids=list(range(8)))
    return np.stack([r["out"] for r in res.results], 0)


# version 06
# speedup vs baseline: 1.0272x; 1.0272x over previous
import numpy as np
import ml_dtypes
import concourse.bass as bass
import concourse.mybir as mybir
from concourse.bass_utils import run_bass_kernel_spmd
from concourse.alu_op_type import AluOpType as OP

F32 = mybir.dt.float32
BF16 = mybir.dt.bfloat16
AF = mybir.ActivationFunctionType
AX = mybir.AxisListType

L, D = 2048, 4096
H, P, NS, G = 128, 64, 128, 8
INTER = H * P
CONVC = INTER + 2 * G * NS
PROJ = INTER + CONVC + H
CS = 128
T = 512
NPASS = L // T
NCH = T // CS
EPS = 1e-5
NCB = PROJ // 128
HPG = H // G


def _inproj_sections():
    out = [("dt", INTER + CONVC, 128)]
    for g in range(G):
        out.append(("B", 2 * INTER + g * NS, NS))
        out.append(("C", 2 * INTER + G * NS + g * NS, NS))
        out.append(("hs", INTER + g * 1024, 1024))
        out.append(("gate", g * 1024, 1024))
    return out


class Sem:
    def __init__(self, h):
        self.h = h
        self.v = 0


class KB:
    ENGS = ("sync", "gpsimd", "vector", "scalar", "tensor")

    def __init__(self):
        self.glob = []
        self.last_wait = {e: {} for e in self.ENGS}

    def emit(self, eng, fn, sem=None, dn=1):
        if sem is None:
            self.glob.append((eng, fn))
            return None
        sem.v += dn
        v = sem.v

        def wrapped(E, fn=fn, s=sem.h, dn=dn):
            fn(E).then_inc(s, dn)

        self.glob.append((eng, wrapped))
        return v

    def wait(self, eng, sem, val):
        if val is None or val <= 0:
            return
        if getattr(self, "paranoid", False) and sem.v > val and sem.h.name in ("SW", "SG"):
            val = sem.v
        assert val <= sem.v, f"{eng}: wait on future milestone {val} > {sem.v}"
        lw = self.last_wait[eng]
        if lw.get(id(sem), -1) >= val:
            return
        lw[id(sem)] = val
        self.glob.append((eng, lambda E, s=sem.h, v=val: E.wait_ge(s, v)))

    def prog_for(self, eng, limit=None):
        items = self.glob if limit is None else self.glob[:limit]
        return [fn for (e, fn) in items if e == eng]


def build_nc(dbg=(), do_build=True, n_pass=NPASS, n_group=G, do_tail=True, limit=None, races=True, paranoid=False):
    nc = bass.Bass("TRN2", target_bir_lowering=False, detect_race_conditions=races)
    dbg = set(dbg)

    x_in = nc.dram_tensor("x", [L, D], F32, kind="ExternalInput").ap()
    ipw = nc.dram_tensor("ipw", [PROJ, D], F32, kind="ExternalInput").ap()
    convw = nc.dram_tensor("convw", [CONVC, 4], F32, kind="ExternalInput").ap()
    convb = nc.dram_tensor("convb", [CONVC], F32, kind="ExternalInput").ap()
    dtb = nc.dram_tensor("dtb", [H], F32, kind="ExternalInput").ap()
    alog = nc.dram_tensor("alog", [H], F32, kind="ExternalInput").ap()
    dvec = nc.dram_tensor("dvec", [H], F32, kind="ExternalInput").ap()
    normw = nc.dram_tensor("normw", [INTER], F32, kind="ExternalInput").ap()
    outw = nc.dram_tensor("outw", [D, INTER], F32, kind="ExternalInput").ap()
    ffw = nc.dram_tensor("ffw", [D, D], F32, kind="ExternalInput").ap()
    ffb = nc.dram_tensor("ffb", [D], F32, kind="ExternalInput").ap()
    ffA = nc.dram_tensor("ffA", [64, D], F32, kind="ExternalInput").ap()
    ffB = nc.dram_tensor("ffB", [D, 64], F32, kind="ExternalInput").ap()
    gw = nc.dram_tensor("gw", [4, D], F32, kind="ExternalInput").ap()
    gb = nc.dram_tensor("gb", [4], F32, kind="ExternalInput").ap()
    out = nc.dram_tensor("out", [L, D], F32, kind="ExternalOutput").ap()

    ipT = nc.dram_tensor("ipT", [NCB, 128, 32, 128], BF16, kind="Internal").ap()
    owT = nc.dram_tensor("owT", [8, 8, 128, 8, 512], BF16, kind="Internal").ap()
    wfT = nc.dram_tensor("wfT", [8, 4, 128, 8, 512], BF16, kind="Internal").ap()
    owP = nc.dram_tensor("owP", [D, INTER], BF16, kind="Internal").ap()
    wfP = nc.dram_tensor("wfP", [D, D], BF16, kind="Internal").ap()
    agT = nc.dram_tensor("agT", [128, 32, 68], BF16, kind="Internal").ap()
    bbT = nc.dram_tensor("bbT", [65, D], BF16, kind="Internal").ap()
    xnScr = nc.dram_tensor("xnScr", [T, D], BF16, kind="Internal").ap()
    hsScr = nc.dram_tensor("hsScr", [1024, T], BF16, kind="Internal").ap()
    bScr = nc.dram_tensor("bScr", [128, T], BF16, kind="Internal").ap()
    gScr = nc.dram_tensor("gScr", [64, 128, T], BF16, kind="Internal").ap()

    dbg_t = {}
    for name, shape in [("xnT", [32, 128, T]), ("cumH", [128, T]), ("proj", [NCB, 128, T]),
                        ("dtH", [128, T]), ("dA", [128, T]), ("dtraw", [128, T]),
                        ("bct", [256, T]), ("xnS", [T, D]), ("lnstat", [NCH, 128, 512]), ("ffng", [NCH, 128, 512]),
                        ("xch", [NCH, 128, 1024]), ("gateT", [8, 128, T]),
                        ("gT", [64, 128, T]), ("gTn", [64, 128, T]),
                        ("cumR", [4, 16, 128]), ("ldec", [4, 128, 128])]:
        if name in dbg:
            dbg_t[name] = nc.dram_tensor("dbg_" + name, shape, F32, kind="ExternalOutput").ap()

    from contextlib import ExitStack
    es = ExitStack()
    S = lambda *a: es.enter_context(nc.sbuf_tensor(*a))
    Ps = lambda *a: es.enter_context(nc.psum_tensor(*a))

    xnT = S("xnT", [128, 32, 512], BF16)
    ggT = S("ggT", [128, 8, 512], BF16)
    gslab = S("gslab", [128, 4, 4, 512], BF16)
    state = S("state", [128, H, 64], F32)
    slab = S("slab", [128, 4, 2048], BF16)
    xLN = S("xLN", [128, 2048], F32)
    xnStg = S("xnStg", [128, 2048], BF16)
    cumH = S("cumH", [128, 512], F32)
    cumT = S("cumT", [128, NCH, 128], F32)
    dtaT = S("dtaT", [128, NCH, 128], F32)
    wTt = S("wTt", [128, NCH, 128], F32)
    ecumT = S("ecumT", [128, NCH, 128], F32)
    ECLs = S("ECLs", [128, NCH, 128], F32)
    dtH = S("dtH", [128, 512], F32)
    zb1 = S("zb1", [128, 512], F32)
    zb2 = S("zb2", [128, 512], F32)
    zb3 = S("zb3", [128, 512], F32)
    BCt = S("BCt", [128, 2, 512], BF16)
    Btok = S("Btok", [128, 2, 128], BF16)
    gateT = S("gateT", [128, 8, 512], BF16)
    hsS = S("hsS", [128, 1, 512], BF16)
    cvX = S("cvX", [128, 1, 515], F32)
    cvH = S("cvH", [128, 80, 3], F32)
    x_ch = S("x_ch", [128, 2, 1024], BF16)
    xwB = S("xwB", [128, 2, 64], BF16)
    prevb = S("prevb", [128, 2, 64], BF16)
    LdR = S("LdR", [128, 128], F32)
    Ldec = S("Ldec", [128, 2, 128], F32)
    CBmS = S("CBmS", [128, 2, 128], BF16)
    scoS = S("scoS", [128, 2, 128], BF16)
    y1B = S("y1B", [128, 64], F32)
    yB = S("yB", [128, 2, 64], F32)
    cumR1 = S("cumR1", [1, 2, 128], F32)
    DBCs = S("DBCs", [128, 128], F32)
    g2b = S("g2b", [128, 512], BF16)
    rstdR = S("rstdR", [1, 512], F32)
    mrgS = S("mrgS", [128, 512], F32)
    mrgO = S("mrgO", [128, 2, 512], F32)
    xmrg = S("xmrg", [128, 2, 512], F32)
    tSf = S("tSf", [128, 64], F32)
    tS2 = S("tS2", [128, 64], F32)
    scGb = S("scGb", [128, 4], F32)
    mxG = S("mxG", [128, 1], F32)
    foG = S("foG", [128, 8], F32)
    tpT = S("tpT", [65, NCH, 128], BF16)
    r65 = S("r65", [65, 2, 512], BF16)
    bbF = S("bbF", [128, 65], F32)
    identf = S("identf", [128, 128], F32)
    ones1 = S("ones1", [1, 128], F32)
    onescb = S("onescb", [128, 1], BF16)
    trilb = S("trilb", [128, 128], BF16)
    rstpat = S("rstpat", [128, 512], BF16)
    onesF = S("onesF", [128, 128], F32)
    cwC = S("cwC", [128, 4, 80], F32)
    convbC = S("convbC", [128, 80], F32)
    normwC = S("normwC", [128, 64], F32)
    dtbC = S("dtbC", [128, 1], F32)
    AcolC = S("AcolC", [128, 1], F32)
    DrowS = S("DrowS", [1, 128], F32)
    gbR = S("gbR", [1, 4], F32)
    GBs = S("GBs", [128, 4], F32)
    explR = S("explR", [1, 128], F32)
    ecol = S("ecol", [128, 1], F32)
    statC = S("statC", [128, 8], F32)
    stats6 = S("stats6", [128, 8, 6], F32)
    meanvar = S("meanvar", [128, 2], F32)
    junkB = S("junkB", [128, 8], BF16)
    epsC = S("epsC", [128, 1], F32)
    oneC = S("oneC", [128, 1], F32)
    ij = S("ij", [128, 128], F32)

    tapS = S("tapS", [128, 512], F32) if dbg else None
    p = [Ps("p%d" % i, [128, 512], F32) for i in range(8)]

    mk = lambda nm: Sem(es.enter_context(nc.semaphore(nm)))
    TE = mk("TE"); VE = mk("VE"); AC = mk("AC"); PO = mk("PO")
    wsl = [mk("wsl%d" % i) for i in range(4)]
    gsl = [mk("gsl%d" % i) for i in range(4)]
    xchS = [mk("xch%d" % i) for i in range(2)]
    btkS = [mk("btk%d" % i) for i in range(2)]
    xmrS = [mk("xmr%d" % i) for i in range(2)]
    r65S = [mk("r65%d" % i) for i in range(2)]
    xlnS = mk("xlnS"); xntS = mk("xntS"); smlS = mk("smlS")
    shsS = mk("shsS"); sxnS = mk("sxnS"); sgrS = mk("sgrS")
    soutS = mk("soutS"); sbldS = mk("sbldS"); tapS2 = mk("tapS2")
    SW = smlS; SG = sbldS

    kb = KB()
    kb.paranoid = paranoid
    bank_guard = [[] for _ in range(8)]

    def bank_w(i):
        g = bank_guard[i]
        bank_guard[i] = []
        return g

    def bank_r(i, sem, ms):
        if ms:
            bank_guard[i].append((sem, ms))

    def dma(eng, dst, src, sem, waits=(), transpose=False):
        for (s, v) in waits:
            kb.wait(eng, s, v)
        if transpose:
            return kb.emit(eng, lambda X, d=dst, s2=src: X.dma_start(d, s2, transpose=True), sem, 16)
        return kb.emit(eng, lambda X, d=dst, s2=src: X.dma_start(d, s2), sem, 16)

    def mm(dst, lhsT, rhs, start, stop, waits=()):
        for (s, v) in waits:
            kb.wait("tensor", s, v)
        return kb.emit("tensor", lambda X, d=dst, l=lhsT, r=rhs, st=start, sp=stop:
                       X.matmul(d, l, r, start=st, stop=sp), TE)

    def tr(dst, src, ident, waits=()):
        for (s, v) in waits:
            kb.wait("tensor", s, v)
        return kb.emit("tensor", lambda X, d=dst, s2=src, i=ident: X.transpose(d, s2, i), TE)

    def vo(fn, waits=()):
        for (s, v) in waits:
            kb.wait("vector", s, v)
        return kb.emit("vector", fn, VE)

    def ao(fn, waits=()):
        for (s, v) in waits:
            kb.wait("scalar", s, v)
        return kb.emit("scalar", fn, AC)

    def po(fn, waits=()):
        for (s, v) in waits:
            kb.wait("gpsimd", s, v)
        return kb.emit("gpsimd", fn, PO)

    tap_guard = [[]]

    def tap(dst_ap, src_ap, waits=()):
        vv = vo(lambda X, d=src_ap: X.tensor_copy(tapS[:, 0:d.shape[-1]] if False else tapS[:], d),
                waits=list(waits) + tap_guard[0])
        dd = dma("gpsimd", dst_ap, tapS[:], tapS2, waits=[(VE, vv)])
        tap_guard[0] = [(tapS2, dd)]

    g_iota = po(lambda X: X.iota(ij[:], [[1, 128]], base=0, channel_multiplier=-1,
                                 allow_small_or_imprecise_dtypes=True))
    v_identf = vo(lambda X: X.tensor_scalar(identf[:], ij[:], 0.0, None, OP.is_equal),
                  waits=[(PO, g_iota)])
    vo(lambda X: X.tensor_scalar(zb1[:, 0:128], ij[:], 0.0, None, OP.is_ge))
    vo(lambda X: X.tensor_copy(trilb[:], zb1[:, 0:128]))
    vo(lambda X: X.memset(ones1[:], 1.0))
    vo(lambda X: X.memset(onesF[:], 1.0))
    vo(lambda X: X.memset(onescb[:], 1.0))
    vo(lambda X: X.memset(rstpat[:], 1.0))
    for c in range(NCH):
        vo(lambda X, c=c: X.memset(rstpat[:, c * CS:c * CS + 1], 0.0))
    vo(lambda X: X.memset(state[:], 0.0))
    vo(lambda X: X.memset(cvH[:], 0.0))
    vo(lambda X: X.memset(epsC[:], EPS))
    vo(lambda X: X.memset(oneC[:], 1.0))
    v_static = vo(lambda X: X.memset(tpT[64:65, :, :], 1.0))

    cwL = xLN[0:80, 0:512]
    cbL = xLN[0:80, 512:640]
    nwL = xLN[0:64, 640:768]
    d_cw = dma("sync", cwL, convw.rearrange("(t p) k -> t (p k)", p=128), smlS)
    d_cb = dma("sync", cbL, convb.rearrange("(t p) -> t p", p=128), smlS)
    d_nw = dma("sync", nwL, normw.rearrange("(t p) -> t p", p=128), smlS)
    d_dtb = dma("sync", dtbC[:], dtb.rearrange("(p one) -> p one", one=1), smlS)
    d_al = dma("sync", ecol[:], alog.rearrange("(p one) -> p one", one=1), smlS)
    d_dr = dma("sync", DrowS[:], dvec.rearrange("(one p) -> one p", one=1), smlS)
    d_gb = dma("sync", gbR[:], gb.rearrange("(one p) -> one p", one=1), smlS)

    t_cw = None
    for k in range(4):
        t_cw = tr(p[2][:, k * 80:(k + 1) * 80],
                  cwL.rearrange("t (part k) -> t k part", k=4)[:, k, :],
                  identf[0:80, 0:80], waits=[(VE, v_identf), (smlS, d_cw)])
    v_cw = vo(lambda X: X.tensor_copy(cwC.rearrange("part k t -> part (k t)"), p[2][:, 0:320]),
              waits=[(TE, t_cw)])
    bank_r(2, VE, v_cw)
    t_cb = tr(p[3][:, 0:80], cbL, identf[0:80, 0:80], waits=[(smlS, d_cb)])
    t_nw = tr(p[3][:, 128:192], nwL, identf[0:64, 0:64], waits=[(smlS, d_nw)])
    v_cb = vo(lambda X: X.tensor_copy(convbC[:], p[3][:, 0:80]), waits=[(TE, t_nw)])
    v_nw = vo(lambda X: X.tensor_copy(normwC[:], p[3][:, 128:192]), waits=[(TE, t_nw)])
    bank_r(3, VE, v_nw)
    a_A = ao(lambda X: X.activation(AcolC[:], ecol[:], AF.Exp), waits=[(smlS, d_al)])
    v_A = vo(lambda X: X.tensor_scalar(AcolC[:], AcolC[:], -1.0, None, OP.mult),
             waits=[(AC, a_A)])
    t_db = mm(p[4][:, 0:128], ones1[:], DrowS[:], True, True,
              waits=[(smlS, d_dr), (VE, v_static)])
    t_gb = mm(p[4][:, 128:132], ones1[:], gbR[:], True, True, waits=[(smlS, d_gb)])
    v_db = vo(lambda X: X.tensor_copy(DBCs[:], p[4][:, 0:128]), waits=[(TE, t_gb)])
    v_gb = vo(lambda X: X.tensor_copy(GBs[:], p[4][:, 128:132]), waits=[(TE, t_gb)])
    bank_r(4, VE, v_gb)

    sections = _inproj_sections()
    cb_rows, cb_kind = [], []
    for (kind, start, n) in sections:
        for i in range(n // 128):
            cb_rows.append(start + i * 128)
            cb_kind.append(kind)
    assert len(cb_rows) == NCB

    ld_guard = [(TE, t_gb), (VE, v_nw), (VE, v_cb), (VE, v_cw)]
    if not do_build:
        ip_build_done = None
        ow_done = wf_done = ag_store = bb_store = None
        slab_guard = [[], []]
    do_build_sections = do_build
    slab_guard = [[], []]
    for cb in (range(NCB) if do_build_sections else []):
        row = cb_rows[cb]
        sp = cb % 2
        ms_cast = None
        for half in range(2):
            d1 = dma("sync", xLN[:, :], ipw[row:row + 128, half * 2048:(half + 1) * 2048],
                     xlnS, waits=ld_guard)
            ld_guard = []
            tms = {}
            for q in range(4):
                for b in range(4):
                    kk = q * 4 + b
                    tms[b] = tr(p[b][:, 0:128], xLN[:, kk * 128:(kk + 1) * 128],
                                identf[:], waits=[(xlnS, d1)] + bank_w(b))
                for b in range(4):
                    kk = half * 16 + q * 4 + b
                    wl = [(TE, tms[b])]
                    if half == 0 and q == 0 and b == 0:
                        wl += slab_guard[sp]
                        slab_guard[sp] = []
                    slot = 2 * sp + (kk // 16)
                    kc = kk % 16
                    ms_cast = vo(lambda X, slot=slot, kc=kc, b=b: X.tensor_copy(
                        slab[:, slot, kc * 128:(kc + 1) * 128], p[b][:, 0:128]), waits=wl)
                    bank_r(b, VE, ms_cast)
            ld_guard = [(TE, tms[3]), (VE, ms_cast)]
        st = dma("gpsimd", ipT[cb][:, 0:16, :],
                 slab[:, 2 * sp, :].rearrange("part (k c) -> part k c", c=128),
                 sbldS, waits=[(VE, ms_cast)])
        st = dma("gpsimd", ipT[cb][:, 16:32, :],
                 slab[:, 2 * sp + 1, :].rearrange("part (k c) -> part k c", c=128),
                 sbldS)
        slab_guard[sp] = [(sbldS, st)]
    ip_build_done = st if do_build_sections else None

    def cast_plain(src, dst, nrows, ncols, ld_guard):
        st = None
        for rt in range(nrows // 128):
            for half in range(ncols // 2048):
                d1 = dma("sync", xLN[:, :],
                         src[rt * 128:(rt + 1) * 128, half * 2048:(half + 1) * 2048],
                         xlnS, waits=ld_guard)
                msc = vo(lambda X: X.tensor_copy(xnStg[:], xLN[:]), waits=[(xlnS, d1)])
                st = dma("gpsimd", dst[rt * 128:(rt + 1) * 128, half * 2048:(half + 1) * 2048],
                         xnStg[:], sbldS, waits=[(VE, msc)])
                ld_guard = [(VE, msc), (sbldS, st)]
        return st, ld_guard

    ow_cast = wf_cast = None
    if do_build_sections:
        ow_cast, ld_guard = cast_plain(outw, owP, D, INTER, ld_guard)
        wf_cast, ld_guard = cast_plain(ffw, wfP, D, D, ld_guard)

    def build_tiled(srcP, dstT, n_ot, n_kq, cast_ms):
        last = None
        for ot in range(n_ot):
            for kq in range(n_kq):
                sp2 = (ot * n_kq + kq) % 2
                ms_l = None
                lastl = {}
                for k in range(8):
                    cblk = kq * 8 + k
                    wts = [(sbldS, cast_ms)]
                    if k == 0:
                        wts += slab_guard[sp2]
                        slab_guard[sp2] = []
                    slot = 2 * sp2 + (k // 4)
                    lastl[slot] = dma("sync", slab[:, slot, (k % 4) * 512:(k % 4 + 1) * 512],
                               srcP[ot * 512:(ot + 1) * 512, cblk * 128:(cblk + 1) * 128],
                               wsl[slot], waits=wts, transpose=True)
                st = dma("gpsimd", dstT[ot, kq][:, 0:4, :],
                         slab[:, 2 * sp2, :].rearrange("part (k o) -> part k o", o=512),
                         sbldS, waits=[(wsl[2 * sp2], lastl[2 * sp2])])
                st = dma("gpsimd", dstT[ot, kq][:, 4:8, :],
                         slab[:, 2 * sp2 + 1, :].rearrange("part (k o) -> part k o", o=512),
                         sbldS, waits=[(wsl[2 * sp2 + 1], lastl[2 * sp2 + 1])])
                slab_guard[sp2] = [(sbldS, st)]
                last = st
        return last

    if do_build_sections:
        ow_done = build_tiled(owP, owT, 8, 8, ow_cast)
        wf_done = build_tiled(wfP, wfT, 8, 4, wf_cast)

    ag_store = None
    ag_guard = ld_guard
    for half in (range(2) if do_build_sections else []):
        dA1 = dma("sync", xLN[0:64, :], ffA[:, half * 2048:(half + 1) * 2048], xlnS,
                  waits=ag_guard)
        dA2 = dma("sync", xLN[64:68, :], gw[:, half * 2048:(half + 1) * 2048], xlnS)
        ag_guard = []
        msc = None
        for kk in range(16):
            k = half * 16 + kk
            wts = [(xlnS, dA2)] + bank_w(5)
            ag_tr = tr(p[5][:, 0:68], xLN[0:68, kk * 128:(kk + 1) * 128],
                       identf[0:68, 0:68], waits=wts)
            msc = vo(lambda X: X.tensor_copy(xnStg[0:128, 0:68], p[5][:, 0:68]),
                     waits=[(TE, ag_tr)] + ([(sbldS, ag_store)] if ag_store else []))
            bank_r(5, VE, msc)
            ag_store = dma("gpsimd", agT[:, k, :], xnStg[0:128, 0:68], sbldS, waits=[(VE, msc)])
        ag_guard = [(TE, ag_tr), (VE, msc)]

    bb_store = None
    bb_guard = []
    for ot2 in (range(32) if do_build_sections else []):
        dB1 = dma("sync", bbF[:, 0:64], ffB[ot2 * 128:(ot2 + 1) * 128, :], smlS, waits=bb_guard)
        dB2 = dma("sync", bbF[:, 64:65],
                  ffb.rearrange("(part one) -> part one", one=1)[ot2 * 128:(ot2 + 1) * 128, :], smlS)
        bb_tr = tr(p[6][0:65, 0:128], bbF[:], identf[:], waits=[(smlS, dB2)] + bank_w(6))
        msc = vo(lambda X: X.tensor_copy(xnStg[0:65, 128:256], p[6][0:65, 0:128]),
                 waits=[(TE, bb_tr)] + ([(sbldS, bb_store)] if bb_store else []))
        bank_r(6, VE, msc)
        bb_store = dma("gpsimd", bbT[:, ot2 * 128:(ot2 + 1) * 128], xnStg[0:65, 128:256],
                       sbldS, waits=[(VE, msc)])
        bb_guard = [(TE, bb_tr)]

    xnT_guard = []
    gT_guard = []
    xLN_guard = bb_guard + ag_guard
    xnStg_guard = [(sbldS, bb_store)] if bb_store else []
    statC_guard = []
    zb_dt_guard = []
    cvx_guard = [[]]
    cvh_guard = {}
    zbc_guard = [[]]
    hss_guard = [[], []]
    bct_guard = [[], []]
    gate_guard = [[] for _ in range(8)]
    xch_guard = [[], []]
    btok_guard = [[], []]
    cbm_guard = [[], []]
    colb_guard = [[], []]
    cumr_guard = [[], []]
    ldec_guard = [[], []]
    sco_guard = [[], []]
    prevb_guard = [[], []]
    xw_guard = [[], []]
    yb_guard = [[], []]
    state_guard = {}
    g2_guard = []
    tsf_guard = []
    xmrg_guard = [[], []]
    mrgs_guard = []
    mrgo_guard = [[], []]
    r65_guard = [[], []]
    gslab_guard = [[], []]
    hsscr_guard = []
    bscr_guard = []

    zbc = [zb1, zb1]
    zbc2 = [zb2, zb2]

    for ps in range(n_pass):
        tok0 = ps * T

        xn_stores = []
        for tc in range(NCH):
            r0 = tok0 + tc * 128
            dx = dma("sync", xLN[:, :], x_in[r0:r0 + 128, 2048:4096], xlnS, waits=xLN_guard)
            xLN_guard = []
            v_sm1 = None
            for q in range(4):
                v_sm1 = vo(lambda X, q=q: X.bn_stats(stats6[:, 4 + q, :],
                                                     xLN[:, q * 512:(q + 1) * 512]),
                           waits=([(xlnS, dx)] + statC_guard) if q == 0 else [])
            statC_guard = []
            dx0 = dma("sync", xLN[:, :], x_in[r0:r0 + 128, 0:2048], xlnS,
                      waits=[(VE, v_sm1)])
            v_bn = None
            for q in range(4):
                v_bn = vo(lambda X, q=q: X.bn_stats(stats6[:, q, :], xLN[:, q * 512:(q + 1) * 512]),
                          waits=[(xlnS, dx0)] if q == 0 else [])
            v6 = vo(lambda X: X.bn_aggr(meanvar[:], stats6.rearrange("part a b -> part (a b)")),
                    waits=[(VE, v_bn)])
            a1 = ao(lambda X: X.activation(statC[:, 6:7], meanvar[:, 1:2], AF.Ln, bias=epsC[:]),
                    waits=[(VE, v6)])
            a2 = ao(lambda X: X.activation(statC[:, 7:8], statC[:, 6:7], AF.Exp, scale=-0.5))
            a_fence = ao(lambda X: X.activation(junkB[0:128, 0:2], statC[:, 6:8], AF.Relu))
            vps0 = vo(lambda X: X.tensor_scalar(xnStg[:], xLN[:], meanvar[:, 0:1], statC[:, 7:8],
                                                OP.subtract, OP.mult),
                      waits=[(AC, a_fence)] + xnStg_guard)
            xnStg_guard = []
            if "lnstat" in dbg:
                vt1 = vo(lambda X: X.tensor_copy(tapS[:, 0:2], meanvar[:]), waits=[(VE, v6)] + tap_guard[0])
                vt2 = vo(lambda X: X.tensor_copy(tapS[:, 2:4], statC[:, 6:8]), waits=[(AC, a_fence)])
                vt3 = vo(lambda X: X.tensor_copy(tapS[:, 4:52], stats6.rearrange("part a b -> part (a b)")))
                td = dma("gpsimd", dbg_t["lnstat"][tc], tapS[:], tapS2, waits=[(VE, vt3)])
                tap_guard[0] = [(tapS2, td)]
            st0 = dma("gpsimd", xnScr[tc * 128:(tc + 1) * 128, 0:2048], xnStg[:], sxnS,
                      waits=[(VE, vps0)])
            dx1 = dma("sync", xLN[:, :], x_in[r0:r0 + 128, 2048:4096], xlnS, waits=[(VE, vps0)])
            vps1 = vo(lambda X: X.tensor_scalar(xnStg[:], xLN[:], meanvar[:, 0:1], statC[:, 7:8],
                                                OP.subtract, OP.mult),
                      waits=[(xlnS, dx1), (sxnS, st0)])
            st1 = dma("gpsimd", xnScr[tc * 128:(tc + 1) * 128, 2048:4096], xnStg[:], sxnS,
                      waits=[(VE, vps1)])
            xn_stores.append(st1)
            xLN_guard = [(VE, vps1)]
            statC_guard = [(VE, vps1)]
            xnStg_guard = [(sxnS, st1)]

        if "xnS" in dbg:
            dma("gpsimd", dbg_t["xnS"], xnScr, tapS2, waits=[(sxnS, xn_stores[-1])])
        ms_xnT = None
        for f in range(32):
            wts = [(sxnS, xn_stores[-1])]
            if f == 0:
                wts += xnT_guard
                xnT_guard = []
            ms_xnT = dma("sync", xnT[:, f, :], xnScr[:, f * 128:(f + 1) * 128], xntS,
                         waits=wts, transpose=True)
        if "xnT" in dbg:
            for f in range(32):
                tap(dbg_t["xnT"][f], xnT[:, f, :], waits=[(xntS, ms_xnT)])

        sp = 0
        wts = [(xntS, ms_xnT)] + slab_guard[sp]
        slab_guard[sp] = []
        d_h0 = dma("sync", slab[:, 2 * sp, :].rearrange("part (k c) -> part k c", c=128),
                   ipT[0][:, 0:16, :], wsl[2 * sp], waits=wts)
        d_h1 = dma("sync", slab[:, 2 * sp + 1, :].rearrange("part (k c) -> part k c", c=128),
                   ipT[0][:, 16:32, :], wsl[2 * sp + 1])
        mmms = None
        wts = [(wsl[2 * sp], d_h0)] + bank_w(0)
        for k in range(32):
            if k == 16:
                wts = [(wsl[2 * sp + 1], d_h1)]
            mmms = mm(p[0][:, 0:512],
                      slab[:, 2 * sp + k // 16, (k % 16) * 128:(k % 16 + 1) * 128],
                      xnT[:, k, :], k == 0, k == 31,
                      waits=wts if k in (0, 16) else [])
        slab_guard[sp] = [(TE, mmms)]
        vz = vo(lambda X: X.tensor_scalar(zb1[:], p[0][:, 0:512], dtbC[:], None, OP.add),
                waits=[(TE, mmms)] + zb_dt_guard)
        zb_dt_guard = []
        bank_r(0, VE, vz)
        aab = ao(lambda X: X.activation(zb2[:], zb1[:], AF.Abs), waits=[(VE, vz)])
        aex = ao(lambda X: X.activation(zb3[:], zb2[:], AF.Exp, scale=-1.0))
        aln = ao(lambda X: X.activation(zb2[:], zb3[:], AF.Ln, bias=oneC[:]))
        arl = ao(lambda X: X.activation(zb3[:], zb1[:], AF.Relu))
        vdt = vo(lambda X: X.tensor_tensor(dtH[:], zb2[:], zb3[:], OP.add), waits=[(AC, arl)])
        vda = vo(lambda X: X.tensor_scalar(zb1[:], dtH[:], AcolC[:], None, OP.mult),
                 waits=[(VE, v_A)])
        vcm = vo(lambda X: X.tensor_tensor_scan(cumH[:], rstpat[:], zb1[:], 0.0, OP.mult, OP.add))
        if "dtraw" in dbg:
            tap(dbg_t["dtraw"], p[0][:, 0:512], waits=[(TE, mmms)])
        if "dtH" in dbg:
            tap(dbg_t["dtH"], dtH[:], waits=[(VE, vdt)])
        if "dA" in dbg:
            tap(dbg_t["dA"], zb1[:], waits=[(VE, vda)])
        chunk_rdy = {}
        e3 = None
        for c in range(NCH):
            adl = ao(lambda X, c=c: X.activation(zb2[:, c * 128:(c + 1) * 128],
                                                 cumH[:, c * 128:(c + 1) * 128], AF.Exp,
                                                 scale=-1.0,
                                                 bias=cumH[:, c * 128 + 127:c * 128 + 128]),
                     waits=[(VE, vcm)])
            vw = vo(lambda X, c=c: X.tensor_tensor(zb3[:, c * 128:(c + 1) * 128],
                                                   dtH[:, c * 128:(c + 1) * 128],
                                                   zb2[:, c * 128:(c + 1) * 128], OP.mult),
                    waits=[(AC, adl)])
            t1 = tr(p[1][:, 0:128], cumH[:, c * 128:(c + 1) * 128], identf[:],
                    waits=[(VE, vcm)] + bank_w(1))
            e1 = vo(lambda X, c=c: X.tensor_copy(cumT[:, c, :], p[1][:, 0:128]), waits=[(TE, t1)])
            bank_r(1, VE, e1)
            t2 = tr(p[1][:, 128:256], dtH[:, c * 128:(c + 1) * 128], identf[:],
                    waits=[(VE, vdt)] + bank_w(1))
            e2 = vo(lambda X, c=c: X.tensor_copy(dtaT[:, c, :], p[1][:, 128:256]), waits=[(TE, t2)])
            bank_r(1, VE, e2)
            t3 = tr(p[1][:, 256:384], zb3[:, c * 128:(c + 1) * 128], identf[:],
                    waits=[(VE, vw)] + bank_w(1))
            e3 = vo(lambda X, c=c: X.tensor_copy(wTt[:, c, :], p[1][:, 256:384]), waits=[(TE, t3)])
            bank_r(1, VE, e3)
            a_ecm = ao(lambda X, c=c: X.activation(ecumT[:, c, :], cumT[:, c, :], AF.Exp),
                       waits=[(VE, e1)])
            ael = ao(lambda X, c=c: X.activation(ecol[:], cumH[:, c * 128 + 127:c * 128 + 128],
                                                 AF.Exp), waits=[(VE, vcm), (VE, v_A)])
            t4 = tr(p[1][0:1, 384:512], ecol[:], identf[:], waits=[(AC, ael)] + bank_w(1))
            e4 = vo(lambda X: X.tensor_copy(explR[:], p[1][0:1, 384:512]), waits=[(TE, t4)])
            bank_r(1, VE, e4)
            t5 = mm(p[5][:, 0:128], ones1[:], explR[:], True, True,
                    waits=[(VE, e4)] + bank_w(5))
            e5 = vo(lambda X, c=c: X.tensor_copy(ECLs[:, c, :], p[5][:, 0:128]), waits=[(TE, t5)])
            bank_r(5, VE, e5)
            chunk_rdy[c] = [(VE, e5), (AC, a_ecm)]
        dt_last = [(VE, e3), (VE, vcm)]
        zbc_guard[0] = [(VE, vw), (VE, vcm)] + ([(TE, t3)] if True else [])
        if "cumH" in dbg:
            tap(dbg_t["cumH"][:, 0:T], cumH[:], waits=[(VE, vcm)])

        cb_idx = 1
        rms_done = None
        lastW_xnT = None
        g_stores = []
        for g in range(n_group):
            conv_outs = {}
            hs_stores = []
            b_store = None
            gate_done = [None] * 8
            for (kind, ntile) in [("B", 1), ("C", 1), ("hs", 8), ("gate", 8)]:
                for it in range(ntile):
                    cb = cb_idx
                    cb_idx += 1
                    sp = cb % 2
                    pp = cb % 2
                    wts = list(slab_guard[sp])
                    slab_guard[sp] = []
                    d_h0 = dma("sync",
                               slab[:, 2 * sp, :].rearrange("part (k c) -> part k c", c=128),
                               ipT[cb][:, 0:16, :], wsl[2 * sp], waits=wts)
                    d_h1 = dma("sync",
                               slab[:, 2 * sp + 1, :].rearrange("part (k c) -> part k c", c=128),
                               ipT[cb][:, 16:32, :], wsl[2 * sp + 1])
                    wts = [(wsl[2 * sp], d_h0)] + bank_w(pp)
                    mmms = None
                    for k in range(32):
                        if k == 16:
                            wts = [(wsl[2 * sp + 1], d_h1)]
                        mmms = mm(p[pp][:, 0:512],
                                  slab[:, 2 * sp + k // 16, (k % 16) * 128:(k % 16 + 1) * 128],
                                  xnT[:, k, :], k == 0, k == 31,
                                  waits=wts if k in (0, 16) else [])
                    slab_guard[sp] = [(TE, mmms)]
                    if "proj" in dbg and ps == 0 and g == 0:
                        tap(dbg_t["proj"][cb], p[pp][:, 0:512], waits=[(TE, mmms)])
                        bank_r(pp, tapS2, tapS2.v)
                    if kind == "gate":
                        asl = ao(lambda X, it=it, pp=pp: X.activation(gateT[:, it, :], p[pp][:, 0:512],
                                                               AF.Silu),
                                 waits=[(TE, mmms)] + gate_guard[it])
                        gate_guard[it] = []
                        bank_r(pp, AC, asl)
                        gate_done[it] = asl
                    else:
                        proj_row = cb_rows[cb]
                        ctg = (proj_row - INTER) // 128
                        bp2 = 0
                        ve1 = vo(lambda X, b2=bp2, ctg=ctg, pp=pp: X.tensor_copy(
                            cvX[:, b2, 3:515], p[pp][:, 0:512]),
                            waits=[(TE, mmms), (VE, v_cb)] + cvx_guard[0])
                        cvx_guard[0] = []
                        bank_r(pp, VE, ve1)
                        ph1 = po(lambda X, b2=bp2, ctg=ctg: X.tensor_copy(
                            cvX[:, b2, 0:3], cvH[:, ctg, :]),
                            waits=[(VE, ve1)] + cvh_guard.get(ctg, []))
                        mac0 = vo(lambda X, b2=bp2, ctg=ctg: X.tensor_scalar(
                            zbc[b2][:, 0:512], cvX[:, b2, 0:512], cwC[:, 0, ctg:ctg + 1],
                            None, OP.mult),
                            waits=[(VE, v_cw), (PO, ph1)] + zbc_guard[0])
                        zbc_guard[0] = []
                        mac1 = vo(lambda X, b2=bp2, ctg=ctg: X.scalar_tensor_tensor(
                            zbc2[b2][:, 0:512], cvX[:, b2, 1:513], cwC[:, 1, ctg:ctg + 1],
                            zbc[b2][:, 0:512], OP.mult, OP.add))
                        mac2 = vo(lambda X, b2=bp2, ctg=ctg: X.scalar_tensor_tensor(
                            zbc[b2][:, 0:512], cvX[:, b2, 2:514], cwC[:, 2, ctg:ctg + 1],
                            zbc2[b2][:, 0:512], OP.mult, OP.add))
                        mac3 = vo(lambda X, b2=bp2, ctg=ctg: X.scalar_tensor_tensor(
                            zbc2[b2][:, 0:512], cvX[:, b2, 3:515], cwC[:, 3, ctg:ctg + 1],
                            zbc[b2][:, 0:512], OP.mult, OP.add))
                        ph2 = po(lambda X, b2=bp2, ctg=ctg: X.tensor_copy(
                            cvH[:, ctg, :], cvX[:, b2, 512:515]), waits=[(VE, mac3)])
                        cvh_guard[ctg] = [(PO, ph2)]
                        cvx_guard[0] = [(VE, mac3), (PO, ph2)]
                        if kind == "hs":
                            asl = ao(lambda X, it=it, b2=bp2, ctg=ctg: X.activation(
                                hsS[:, 0, :], zbc2[b2][:, 0:512], AF.Silu,
                                bias=convbC[:, ctg:ctg + 1]),
                                waits=[(VE, mac3)] + hss_guard[0])
                            hss_guard[0] = []
                            zbc_guard[0] = [(AC, asl)]
                            sth = dma("gpsimd", hsScr[it * 128:(it + 1) * 128, :],
                                      hsS[:, 0, :], shsS,
                                      waits=[(AC, asl)] + (hsscr_guard if it == 0 else []))
                            if it == 0:
                                hsscr_guard = []
                            hs_stores.append(sth)
                            hss_guard[0] = [(shsS, sth)]
                        else:
                            bc = 0 if kind == "B" else 1
                            asl = ao(lambda X, bc=bc, b2=bp2, ctg=ctg: X.activation(
                                BCt[:, bc, :], zbc2[b2][:, 0:512], AF.Silu,
                                bias=convbC[:, ctg:ctg + 1]),
                                waits=[(VE, mac3)] + bct_guard[bc])
                            bct_guard[bc] = []
                            zbc_guard[0] = [(AC, asl)]
                            conv_outs[kind] = asl
                            if kind == "B":
                                b_store = dma("gpsimd", bScr[:], BCt[:, 0, :], shsS,
                                              waits=[(AC, asl)] + bscr_guard)
                                bscr_guard = []
            if "bct" in dbg and g == 0 and ps == 0:
                tap(dbg_t["bct"][0:128], BCt[:, 0, :], waits=[(AC, conv_outs["B"])])
                tap(dbg_t["bct"][128:256], BCt[:, 1, :], waits=[(AC, conv_outs["C"])])
            if "gateT" in dbg and g == 0 and ps == 0:
                for it in range(8):
                    tap(dbg_t["gateT"][it], gateT[:, it, :], waits=[(AC, gate_done[it])])

            last_gT = None
            last_dx = last_db = None
            last_tyd = last_pxw = last_vy1 = last_tst = None
            for c in range(NCH):
                cpar = c % 2
                wts = [(shsS, hs_stores[-1])] + xch_guard[cpar]
                xch_guard[cpar] = []
                d_x = dma("sync", x_ch[:, cpar, :], hsScr[:, c * 128:(c + 1) * 128], xchS[cpar],
                          waits=wts, transpose=True)
                wts = [(shsS, b_store)] + btok_guard[cpar]
                btok_guard[cpar] = []
                d_b = dma("sync", Btok[:, cpar, :], bScr[:, c * 128:(c + 1) * 128], btkS[cpar],
                          waits=wts, transpose=True)
                last_dx, last_db = d_x, d_b
                if "xch" in dbg and g == 0 and ps == 0:
                    tap(dbg_t["xch"][c, :, 0:512], x_ch[:, cpar, 0:512], waits=[(xchS[cpar], d_x)])
                    tap(dbg_t["xch"][c, :, 512:1024], x_ch[:, cpar, 512:1024], waits=[(xchS[cpar], d_x)])
                last_tR = None
                wcb = [(AC, conv_outs["C"]), (AC, conv_outs["B"])] + bank_w(2)
                t_cbm = mm(p[2][:, 0:128], BCt[:, 0, c * 128:(c + 1) * 128],
                           BCt[:, 1, c * 128:(c + 1) * 128], True, True, waits=wcb)
                v_cbm = vo(lambda X, cp=cpar: X.tensor_tensor(CBmS[:, cp, :], p[2][:, 0:128],
                                                              trilb[:], OP.mult),
                           waits=[(TE, t_cbm)] + cbm_guard[cpar])
                cbm_guard[cpar] = []
                bank_r(2, VE, v_cbm)
                for hl in range(HPG):
                    h = g * HPG + hl
                    hp = hl % 2
                    bR = 3 if hp == 0 else 7
                    t_ch = tr(p[bR][0:1, 256:384], cumT[:, c, h:h + 1], identf[:],
                              waits=chunk_rdy[c] + bank_w(bR))
                    e_ch = vo(lambda X, hp=hp, bR=bR: X.tensor_copy(cumR1[0:1, hp, :],
                                                                    p[bR][0:1, 256:384]),
                              waits=[(TE, t_ch)] + cumr_guard[hp])
                    cumr_guard[hp] = []
                    bank_r(bR, VE, e_ch)
                    t_R = mm(p[bR][:, 0:128], ones1[:], cumR1[0:1, hp, :], True, True,
                             waits=[(VE, e_ch)] + bank_w(bR))
                    cumr_guard[hp] = [(TE, t_R)]
                    last_tR = t_R
                    a_r = ao(lambda X, c=c, h=h, bR=bR: X.activation(LdR[:], p[bR][:, 0:128], AF.Relu,
                                                              scale=-1.0,
                                                              bias=cumT[:, c, h:h + 1]),
                             waits=[(TE, t_R)])
                    bank_r(bR, AC, a_r)
                    a_e = ao(lambda X, hp=hp: X.activation(Ldec[:, hp, :], LdR[:], AF.Exp,
                                                           scale=-1.0), waits=ldec_guard[hp])
                    ldec_guard[hp] = []
                    v_sc = vo(lambda X, hp=hp, cp=cpar, c=c, h=h: X.scalar_tensor_tensor(
                        scoS[:, hp, :], CBmS[:, cp, :], dtaT[:, c, h:h + 1], Ldec[:, hp, :],
                        OP.mult, OP.mult),
                        waits=[(AC, a_e), (VE, v_cbm)] + sco_guard[hp])
                    sco_guard[hp] = []
                    ldec_guard[hp] = [(VE, v_sc)]
                    if "ldec" in dbg and g == 0 and ps == 0 and c == 0 and hl < 4:
                        vv_ld = vo(lambda X, hp=hp: X.tensor_copy(tapS[:, 0:128], Ldec[:, hp, :]),
                                   waits=[(AC, a_e)] + tap_guard[0])
                        dd_ld = dma("gpsimd", dbg_t["ldec"][hl], tapS[:, 0:128], tapS2,
                                    waits=[(VE, vv_ld)])
                        tap_guard[0] = [(tapS2, dd_ld)]
                        ldec_guard[hp] = [(VE, v_sc), (tapS2, dd_ld)]
                    t_yd = mm(p[4][:, 0:64], scoS[:, hp, :],
                              x_ch[:, cpar, hl * 64:(hl + 1) * 64], True, True,
                              waits=[(VE, v_sc), (xchS[cpar], d_x)] + bank_w(4))
                    sco_guard[hp] = [(TE, t_yd)]
                    last_tyd = t_yd
                    p_pv = po(lambda X, hp=hp, h=h: X.tensor_copy(prevb[:, hp, :], state[:, h, :]),
                              waits=prevb_guard[hp] + state_guard.get(h, []))
                    prevb_guard[hp] = []
                    t_yi = mm(p[5][:, 0:64], BCt[:, 1, c * 128:(c + 1) * 128], prevb[:, hp, :],
                              True, True, waits=[(PO, p_pv)] + bank_w(5))
                    prevb_guard[hp] = [(TE, t_yi)]
                    p_xw = po(lambda X, hp=hp, cp=cpar, c=c, h=h, hl=hl: X.tensor_scalar(
                        xwB[:, hp, :], x_ch[:, cp, hl * 64:(hl + 1) * 64], wTt[:, c, h:h + 1],
                        None, OP.mult), waits=xw_guard[hp] + [(xchS[cpar], d_x)])
                    xw_guard[hp] = []
                    last_pxw = p_xw
                    t_st = mm(p[6][:, 0:64], Btok[:, cpar, :], xwB[:, hp, :], True, True,
                              waits=[(PO, p_xw), (btkS[cpar], d_b)] + bank_w(6))
                    xw_guard[hp] = [(TE, t_st)]
                    last_tst = t_st
                    v_su = vo(lambda X, h=h, c=c: X.scalar_tensor_tensor(
                        state[:, h, :], state[:, h, :], ECLs[:, c, h:h + 1], p[6][:, 0:64],
                        OP.mult, OP.add), waits=[(TE, t_st), (PO, p_pv)])
                    bank_r(6, VE, v_su)
                    state_guard[h] = [(VE, v_su)]
                    v_y1 = vo(lambda X, cp=cpar, hl=hl, h=h: X.scalar_tensor_tensor(
                        y1B[:], x_ch[:, cp, hl * 64:(hl + 1) * 64], DBCs[:, h:h + 1],
                        p[4][:, 0:64], OP.mult, OP.add), waits=[(TE, t_yd), (VE, v_db)])
                    bank_r(4, VE, v_y1)
                    last_vy1 = v_y1
                    v_y = vo(lambda X, hp=hp, c=c, h=h: X.scalar_tensor_tensor(
                        yB[:, hp, :], p[5][:, 0:64], ecumT[:, c, h:h + 1], y1B[:],
                        OP.mult, OP.add), waits=[(TE, t_yi)] + yb_guard[hp])
                    yb_guard[hp] = []
                    bank_r(5, VE, v_y)
                    t_yt = tr(p[bR][0:64, 128:256], yB[:, hp, :], identf[:],
                              waits=[(VE, v_y)] + bank_w(bR))
                    yb_guard[hp] = [(TE, t_yt)]
                    gtile = hl // 2
                    goff = (hl % 2) * 64
                    v_g = vo(lambda X, gt=gtile, go=goff, hl=hl, c=c, bR=bR: X.tensor_tensor(
                        ggT[go:go + 64, gt, c * 128:(c + 1) * 128], p[bR][0:64, 128:256],
                        gateT[go:go + 64, hl // 2, c * 128:(c + 1) * 128], OP.mult),
                        waits=[(TE, t_yt), (AC, gate_done[hl // 2])] + gT_guard)
                    gT_guard = []
                    bank_r(bR, VE, v_g)
                    last_gT = v_g
                cbm_guard[cpar] = [(VE, v_sc)]
                cumr_guard[cpar] = [(TE, last_tR)]
                xch_guard[cpar] = [(TE, last_tyd), (PO, last_pxw), (VE, last_vy1)]
                btok_guard[cpar] = [(TE, last_tst)]
            hsscr_guard = [(xchS[0], xchS[0].v), (xchS[1], xchS[1].v)]
            bscr_guard = [(btkS[0], btkS[0].v), (btkS[1], btkS[1].v)]
            for it in range(8):
                gate_guard[it] = [(VE, last_gT)]
            bct_guard[0] = [(TE, last_tst)]
            bct_guard[1] = [(TE, max(t_cbm, t_yi))]

            t_v = None
            for ct in range(8):
                a_g2 = ao(lambda X, g=g, ct=ct: X.activation(g2b[:], ggT[:, ct, :],
                                                             AF.Square),
                          waits=[(VE, last_gT)] + g2_guard)
                g2_guard = []
                t_v = mm(p[2][0:1, 0:512], onescb[:], g2b[:], ct == 0, ct == 7,
                         waits=[(AC, a_g2)] + (bank_w(2) if ct == 0 else []))
                g2_guard = [(TE, t_v)]
            a_ln = ao(lambda X: X.activation(rstdR[:], p[2][0:1, 0:512], AF.Ln,
                                             scale=1.0 / 1024.0, bias=epsC[0:1, :]), waits=[(TE, t_v)])
            a_rs = ao(lambda X: X.activation(rstdR[:], rstdR[:], AF.Exp,
                                             scale=-0.5))
            bank_r(2, AC, a_ln)
            t_rb = mm(p[2][:, 0:512], ones1[:], rstdR[:], True, True,
                      waits=[(AC, a_rs)] + bank_w(2))
            vms = None
            for ct in range(8):
                vms = vo(lambda X, g=g, ct=ct: X.scalar_tensor_tensor(
                    ggT[:, ct, :], ggT[:, ct, :],
                    normwC[:, g * 8 + ct:g * 8 + ct + 1], p[2][:, 0:512], OP.mult, OP.mult),
                    waits=[(TE, t_rb), (VE, v_nw)])
            bank_r(2, VE, vms)
            rms_done = vms
            st_g = dma("gpsimd", gScr.rearrange("t part f -> part t f")[:, g * 8:(g + 1) * 8, :],
                       ggT[:], sgrS, waits=[(VE, vms)])
            g_stores.append(st_g)
            gT_guard = [(sgrS, st_g)]
            if "gT" in dbg and ps == 0:
                for ct in range(8):
                    tap(dbg_t["gT"][g * 8 + ct], ggT[:, ct, :], waits=[(VE, vms)])

        if not do_tail:
            continue
        wts = list(slab_guard[0]) + list(slab_guard[1])
        slab_guard[0] = []
        slab_guard[1] = []
        d_ag0 = dma("sync", slab[:, 0, 0:1088],
                    agT[:, 0:16, :].rearrange("part k c -> part (k c)"), wsl[0],
                    waits=wts + ([(sbldS, ag_store)] if ag_store else []))
        d_agl = dma("sync", slab[:, 1, 0:1088],
                    agT[:, 16:32, :].rearrange("part k c -> part (k c)"), wsl[1])
        gat_ms = {}
        last_gmm = None
        for tc in range(NCH):
            wts = [(wsl[0], d_ag0), (wsl[1], d_agl)] + bank_w(0)
            mmt = None
            for k in range(32):
                agsl = slab[:, k // 16, (k % 16) * 68:(k % 16 + 1) * 68]
                mmt = mm(p[0][:, 0:68], xnT[:, k, tc * 128:(tc + 1) * 128], agsl,
                         k == 0, k == 31, waits=wts if k == 0 else [])
            last_gmm = mmt
            v_t = vo(lambda X: X.tensor_copy(tSf[:], p[0][:, 0:64]),
                     waits=[(TE, mmt)] + tsf_guard)
            tsf_guard = []
            v_sg = vo(lambda X: X.tensor_tensor(scGb[:], p[0][:, 64:68], GBs[:], OP.add),
                      waits=[(VE, v_gb)])
            bank_r(0, VE, v_sg)
            v_mx = vo(lambda X: X.tensor_reduce(mxG[:], scGb[:], AX.X, OP.max),
                      waits=[(VE, v_sg)])
            pv = vo(lambda X: X.tensor_tensor(foG[:, 0:1], scGb[:, 0:1], mxG[:], OP.is_ge),
                    waits=[(VE, v_mx)])
            pv = vo(lambda X: X.tensor_scalar(foG[:, 4:5], foG[:, 0:1], -1.0, 1.0, OP.mult, OP.add),
                    waits=[(VE, pv)])
            pv = vo(lambda X: X.tensor_tensor(foG[:, 1:2], scGb[:, 1:2], mxG[:], OP.is_ge),
                    waits=[(VE, pv)])
            pv = vo(lambda X: X.tensor_tensor(foG[:, 1:2], foG[:, 1:2], foG[:, 4:5], OP.mult),
                    waits=[(VE, pv)])
            pv = vo(lambda X: X.tensor_tensor(foG[:, 5:6], foG[:, 4:5], foG[:, 1:2], OP.subtract),
                    waits=[(VE, pv)])
            pv = vo(lambda X: X.tensor_tensor(foG[:, 2:3], scGb[:, 2:3], mxG[:], OP.is_ge),
                    waits=[(VE, pv)])
            pv = vo(lambda X: X.tensor_tensor(foG[:, 2:3], foG[:, 2:3], foG[:, 5:6], OP.mult),
                    waits=[(VE, pv)])
            pv = vo(lambda X: X.tensor_tensor(foG[:, 6:7], foG[:, 5:6], foG[:, 2:3], OP.subtract),
                    waits=[(VE, pv)])
            vo(lambda X: X.tensor_copy(tS2[:, 0:8], tSf[:, 0:8]), waits=[(VE, pv)])
            vo(lambda X: X.tensor_scalar(tS2[:, 8:16], tSf[:, 8:16], foG[:, 4:5], None, OP.mult))
            vo(lambda X: X.tensor_scalar(tS2[:, 16:32], tSf[:, 16:32], foG[:, 5:6], None, OP.mult))
            v_m3 = vo(lambda X: X.tensor_scalar(tS2[:, 32:64], tSf[:, 32:64], foG[:, 6:7],
                                                None, OP.mult))
            tsf_guard = [(VE, v_m3)]
            if "ffng" in dbg and ps == 0:
                f1 = vo(lambda X: X.tensor_copy(tapS[:, 0:4], scGb[:]), waits=tap_guard[0])
                vo(lambda X: X.tensor_copy(tapS[:, 4:5], mxG[:]))
                vo(lambda X: X.tensor_copy(tapS[:, 8:16], foG[:]))
                vo(lambda X: X.tensor_copy(tapS[:, 16:80], tSf[:]))
                f5 = vo(lambda X: X.tensor_copy(tapS[:, 80:144], tS2[:]))
                td = dma("gpsimd", dbg_t["ffng"][tc], tapS[:], tapS2, waits=[(VE, f5)])
                tap_guard[0] = [(tapS2, td)]
            t_tp = tr(p[3][0:64, 256:384], tS2[:], identf[:], waits=[(VE, v_m3)] + bank_w(3))
            v_tp = vo(lambda X, tc=tc: X.tensor_copy(tpT[0:64, tc, :], p[3][0:64, 256:384]),
                      waits=[(TE, t_tp)])
            bank_r(3, VE, v_tp)
            gat_ms[tc] = v_tp
        slab_guard[0] = [(TE, last_gmm)]
        slab_guard[1] = [(TE, last_gmm)]

        for ot in range(8):
            xmrg_ms = {}
            for tc in range(2):
                wts = list(xmrg_guard[tc])
                xmrg_guard[tc] = []
                xmrg_ms[tc] = dma("sync", xmrg[:, tc, :],
                                  x_in[tok0 + tc * 128:tok0 + (tc + 1) * 128,
                                       ot * 512:(ot + 1) * 512], xmrS[tc], waits=wts)
            mmo = None
            for kq in range(8):
                sp2 = (ot * 13 + kq) % 2
                sp3 = kq % 2
                wts = list(slab_guard[sp2])
                slab_guard[sp2] = []
                d_u0 = dma("sync",
                           slab[:, 2 * sp2, :].rearrange("part (k o) -> part k o", o=512),
                           owT[ot, kq][:, 0:4, :], wsl[2 * sp2],
                           waits=wts + ([(sbldS, ow_done)] if ow_done else []))
                d_u = dma("sync",
                          slab[:, 2 * sp2 + 1, :].rearrange("part (k o) -> part k o", o=512),
                          owT[ot, kq][:, 4:8, :], wsl[2 * sp2 + 1])
                wts = list(gslab_guard[sp3])
                gslab_guard[sp3] = []
                d_g0 = dma("sync", gslab[:, 2 * sp3, :, :],
                           gScr.rearrange("t part f -> part t f")[:, kq * 8:kq * 8 + 4, :],
                           gsl[2 * sp3], waits=wts + [(sgrS, g_stores[-1])])
                d_g = dma("sync", gslab[:, 2 * sp3 + 1, :, :],
                          gScr.rearrange("t part f -> part t f")[:, kq * 8 + 4:kq * 8 + 8, :],
                          gsl[2 * sp3 + 1])
                for k8 in range(8):
                    for tc in range(NCH):
                        wl = [((wsl[2 * sp2], d_u0) if k8 < 4 else (wsl[2 * sp2 + 1], d_u)), ((gsl[2 * sp3], d_g0) if k8 < 4 else (gsl[2 * sp3 + 1], d_g))]
                        if kq == 0 and k8 == 0:
                            wl += bank_w(tc)
                        mmo = mm(p[tc][:, 0:512],
                                 gslab[:, 2 * sp3 + k8 // 4, k8 % 4, tc * 128:(tc + 1) * 128],
                                 slab[:, 2 * sp2 + k8 // 4, (k8 % 4) * 512:(k8 % 4 + 1) * 512],
                                 kq == 0 and k8 == 0, kq == 7 and k8 == 7, waits=wl)
                slab_guard[sp2] = [(TE, mmo)]
                gslab_guard[sp3] = [(TE, mmo)]
            mmw = None
            for kq in range(4):
                sp2 = (ot * 13 + 8 + kq) % 2
                wts = list(slab_guard[sp2])
                slab_guard[sp2] = []
                d_u0 = dma("sync",
                           slab[:, 2 * sp2, :].rearrange("part (k o) -> part k o", o=512),
                           wfT[ot, kq][:, 0:4, :], wsl[2 * sp2],
                           waits=wts + ([(sbldS, wf_done)] if wf_done else []))
                d_u = dma("sync",
                          slab[:, 2 * sp2 + 1, :].rearrange("part (k o) -> part k o", o=512),
                          wfT[ot, kq][:, 4:8, :], wsl[2 * sp2 + 1])
                for k8 in range(8):
                    kk = kq * 8 + k8
                    for tc in range(NCH):
                        wl = [((wsl[2 * sp2], d_u0) if k8 < 4 else (wsl[2 * sp2 + 1], d_u))]
                        if kq == 0 and k8 == 0:
                            wl += bank_w(4 + tc)
                        mmw = mm(p[4 + tc][:, 0:512], xnT[:, kk, tc * 128:(tc + 1) * 128],
                                 slab[:, 2 * sp2 + k8 // 4, (k8 % 4) * 512:(k8 % 4 + 1) * 512],
                                 kq == 0 and k8 == 0, False, waits=wl)
                slab_guard[sp2] = [(TE, mmw)]
            lastW_xnT = mmw
            rp = ot % 2
            wts = list(r65_guard[rp])
            r65_guard[rp] = []
            d_r = dma("sync", r65[:, rp, :], bbT[:, ot * 512:(ot + 1) * 512], r65S[rp],
                      waits=wts + [(sbldS, bb_store)])
            mml = None
            for tc in range(NCH):
                mml = mm(p[4 + tc][:, 0:512], tpT[:, tc, :], r65[:, rp, :], False, True,
                         waits=[(r65S[rp], d_r), (VE, gat_ms[tc])])
            r65_guard[rp] = [(TE, mml)]
            for tc in range(NCH):
                mp = tc % 2
                a_si = ao(lambda X, tc=tc: X.activation(mrgS[:], p[4 + tc][:, 0:512], AF.Silu),
                          waits=[(TE, mml)] + mrgs_guard)
                mrgs_guard = []
                bank_r(4 + tc, AC, a_si)
                v_mu = vo(lambda X, tc=tc, mp=mp: X.tensor_tensor(
                    mrgO[:, mp, :], p[tc][:, 0:512], mrgS[:], OP.mult),
                    waits=[(AC, a_si), (TE, mmo)] + mrgo_guard[mp])
                mrgo_guard[mp] = []
                bank_r(tc, VE, v_mu)
                v_ad = vo(lambda X, tc=tc, mp=mp: X.tensor_tensor(
                    mrgO[:, mp, :], mrgO[:, mp, :], xmrg[:, tc % 2, :], OP.add),
                    waits=[(xmrS[tc % 2], xmrg_ms[tc])])
                if tc + 2 < NCH:
                    xmrg_ms[tc + 2] = dma("sync", xmrg[:, tc % 2, :],
                                          x_in[tok0 + (tc + 2) * 128:tok0 + (tc + 3) * 128,
                                               ot * 512:(ot + 1) * 512], xmrS[tc % 2],
                                          waits=[(VE, v_ad)])
                mrgs_guard = [(VE, v_mu)]
                st_o = dma("gpsimd", out[tok0 + tc * 128:tok0 + (tc + 1) * 128,
                                         ot * 512:(ot + 1) * 512],
                           mrgO[:, mp, :], soutS, waits=[(VE, v_ad)])
                mrgo_guard[mp] = [(soutS, st_o)]
                if tc >= 2:
                    xmrg_guard[tc % 2] = [(VE, v_ad)]
        xnT_guard = [(TE, lastW_xnT)]
        gT_guard = [(gsl[i], gsl[i].v) for i in range(4)]
        zb_dt_guard = [(PO, PO.v), (AC, AC.v)]

    kb.wait("gpsimd", soutS, soutS.v)
    kb.wait("gpsimd", sgrS, sgrS.v)
    kb.wait("gpsimd", sxnS, sxnS.v)
    kb.wait("gpsimd", shsS, shsS.v)
    kb.wait("gpsimd", sbldS, sbldS.v)

    with es, nc.Block() as block:
        @block.sync
        def _(E):
            for fn in kb.prog_for("sync", limit):
                fn(E)

        @block.gpsimd
        def _(E):
            for fn in kb.prog_for("gpsimd", limit):
                fn(E)

        @block.vector
        def _(E):
            for fn in kb.prog_for("vector", limit):
                fn(E)

        @block.scalar
        def _(E):
            for fn in kb.prog_for("scalar", limit):
                fn(E)

        @block.tensor
        def _(E):
            for fn in kb.prog_for("tensor", limit):
                fn(E)

    return nc


def kernel(**inputs):
    nc = build_nc()
    x = np.ascontiguousarray(inputs["L0_attn_input_states"])
    base = {
        "ipw": np.ascontiguousarray(inputs["L0_attn_in_proj_w"]),
        "convw": np.ascontiguousarray(inputs["L0_attn_conv_w"][:, 0, :]),
        "convb": np.ascontiguousarray(inputs["L0_attn_conv_b"]),
        "dtb": np.ascontiguousarray(inputs["L0_attn_dt_bias"]),
        "alog": np.ascontiguousarray(inputs["L0_attn_A_log"]),
        "dvec": np.ascontiguousarray(inputs["L0_attn_D"]),
        "normw": np.ascontiguousarray(inputs["L0_attn_norm_w"]),
        "outw": np.ascontiguousarray(inputs["L0_attn_out_w"]),
        "ffw": np.ascontiguousarray(inputs["L0_ffn_W"]),
        "ffb": np.ascontiguousarray(inputs["L0_ffn_b"]),
        "ffA": np.ascontiguousarray(inputs["L0_ffn_A"]),
        "ffB": np.ascontiguousarray(inputs["L0_ffn_B"]),
        "gw": np.ascontiguousarray(inputs["L0_ffn_gate_w"]),
        "gb": np.ascontiguousarray(inputs["L0_ffn_gate_b"]),
    }
    in_maps = []
    for b in range(8):
        m = dict(base)
        m["x"] = np.ascontiguousarray(x[b])
        in_maps.append(m)
    res = run_bass_kernel_spmd(nc, in_maps, core_ids=list(range(8)))
    return np.stack([r["out"] for r in res.results], 0)
